# version 03
# baseline (speedup 1.0000x reference)
import os
import numpy as np
import ml_dtypes
from contextlib import ExitStack
import concourse.bass as bass
import concourse.mybir as mybir
from concourse.bass_utils import run_bass_kernel_spmd

F32 = mybir.dt.float32
BF16 = mybir.dt.bfloat16
I32 = mybir.dt.int32
AF = mybir.ActivationFunctionType
ALU = mybir.AluOpType
AXX = mybir.AxisListType.X

B, S, D = 4, 2048, 2048
TPC = 1024
NCORES = 8
H0, DN, DR, DV = 16, 128, 64, 128
QLR, KVLR = 1536, 512
DKV = KVLR + DR
H1, HD1 = 32, 64
I0 = 8192
G, GD, NACT = 4096, 256, 32
EPS_LN = 1e-5
EPS_RMS0 = 1e-6
EPS_RMS1 = float(np.finfo(np.float32).eps)
NBLK, BLKT = 4, 512
SC0 = 1.0 / float(np.sqrt(DN + DR))
SC1 = 1.0 / float(np.sqrt(HD1))
FLAGBASE = 0x5A5A0000

ENGINES = ("sync", "tensor", "vector", "scalar", "gpsimd")
QSEM = {"sync": "dsy", "gpsimd": "dgp", "scalar": "dsc"}
PSEM = {"tensor": "ten", "vector": "vec", "scalar": "sca", "gpsimd": "pol",
        "sync": "syp"}

OWNPOS = {0: [0, 3], 1: [1, 2]}
GPOS = [0, 3, 1, 2]
GSLOT = {pp: GPOS.index(pp) for pp in range(NBLK)}


class Buf:
    __slots__ = ("name", "w", "r", "psum")

    def __init__(self, name, psum=False):
        self.name = name
        self.w = None
        self.r = []
        self.psum = psum


NLANES = 4


class Prog:
    def __init__(self, nc):
        self.nc = nc
        self.q = {e: [] for e in ENGINES}
        self.semnames = sorted(PSEM.values()) + ["rdv"]
        for qs in QSEM.values():
            self.semnames += [f"{qs}{i}" for i in range(NLANES)]
        self.cnt = {s: 0 for s in self.semnames}
        self.rr = {e: 0 for e in QSEM}
        self.hwm = {e: {} for e in ENGINES}
        self.bufs = []
        self._fork = None
        self.simsafe = bool(int(os.environ.get("BASS_SIMSAFE", "0")))

    def buf(self, name, psum=False):
        b = Buf(name, psum)
        self.bufs.append(b)
        return b

    def _emit_waits(self, eng, conds):
        own = PSEM[eng]
        todo = []
        for c in conds:
            if c is None:
                continue
            s, v = c
            if v <= 0:
                continue
            if s == own and not self.simsafe:
                continue
            if self.hwm[eng].get(s, 0) >= v:
                continue
            self.hwm[eng][s] = v
            todo.append((s, v))
        if todo:
            def f(h, sems, todo=tuple(todo)):
                for s, v in todo:
                    h.wait_ge(sems[s], v)
            self.q[eng].append(f)

    def op(self, eng, maker, reads=(), writes=(), params=(), sem=None, inc=1):
        conds = [b.w for b in reads]
        for b in reads:
            if b.psum:
                conds.extend(b.r)
        for b in params:
            if b.w is not None:
                s, v = b.w
                if s == PSEM[eng]:
                    if self.hwm[eng].get(s, 0) < v:
                        self.hwm[eng][s] = v
                        def ff(h, sems, s=s, v=v):
                            h.wait_ge(sems[s], v)
                        self.q[eng].append(ff)
                else:
                    conds.append(b.w)
        for b in writes:
            conds.append(b.w)
            conds.extend(b.r)
        if os.environ.get("TRACEBUF") and any(
                b.name == os.environ["TRACEBUF"] for b in writes):
            print(f"[trace] {eng} op writes {os.environ['TRACEBUF']} "
                  f"conds={[c for c in conds if c]} hwm={dict(self.hwm[eng])}")
        self._emit_waits(eng, conds)
        if sem is None:
            sem = PSEM[eng]
        self.cnt[sem] += inc
        cnt = self.cnt[sem]
        def f(h, sems, maker=maker, sem=sem, inc=inc):
            maker(h, sems).then_inc(sems[sem], inc)
        self.q[eng].append(f)
        tok = (sem, cnt)
        for b in tuple(reads) + tuple(params):
            b.r.append(tok)
        for b in writes:
            b.w = tok
            b.r = []
        return tok

    def dma(self, eng, out, in_, reads=(), writes=(), **kw):
        lane = self.rr[eng] % NLANES
        self.rr[eng] += 1
        sem = f"{QSEM[eng]}{lane}"
        prev = self.cnt[sem]
        if prev > 0:
            self._emit_waits(eng, [(sem, prev)])
        return self.op(eng, lambda h, s: h.dma_start(out=out, in_=in_, **kw),
                       reads=reads, writes=writes, sem=sem, inc=16)

    def raw(self, eng, fn):
        self.q[eng].append(fn)

    def barrier(self, bufs):
        snap = [(s, self.cnt[s]) for s in self.semnames if s != "rdv"]
        for b in bufs:
            for tok in snap:
                b.r.append(tok)

    def fork(self):
        assert self._fork is None
        self._fork = {
            "qlen": {e: len(self.q[e]) for e in ENGINES},
            "cnt": dict(self.cnt),
            "rr": dict(self.rr),
            "hwm": {e: dict(self.hwm[e]) for e in ENGINES},
            "bufs": {id(b): (b.w, list(b.r)) for b in self.bufs},
        }

    def alt(self):
        fk = self._fork
        fk["qa"] = {e: self.q[e][fk["qlen"][e]:] for e in ENGINES}
        for e in ENGINES:
            del self.q[e][fk["qlen"][e]:]
        fk["cnt_a"] = dict(self.cnt)
        fk["rr_a"] = dict(self.rr)
        fk["hwm_a"] = {e: dict(self.hwm[e]) for e in ENGINES}
        fk["bufs_a"] = {id(b): (b.w, list(b.r)) for b in self.bufs}
        self.cnt = dict(fk["cnt"])
        self.rr = dict(fk["rr"])
        self.hwm = {e: dict(fk["hwm"][e]) for e in ENGINES}
        for b in self.bufs:
            if id(b) in fk["bufs"]:
                b.w, b.r = fk["bufs"][id(b)][0], list(fk["bufs"][id(b)][1])

    def join(self, preg):
        fk = self._fork
        self._fork = None
        assert self.cnt == fk["cnt_a"], \
            f"branch sem counts differ: {self.cnt} vs {fk['cnt_a']}"
        assert self.rr == fk["rr_a"]
        qb = {e: self.q[e][fk["qlen"][e]:] for e in ENGINES}
        for e in ENGINES:
            del self.q[e][fk["qlen"][e]:]
        for e in ENGINES:
            qa, qbb = fk["qa"][e], qb[e]
            if not qa and not qbb:
                continue
            def wrap(h, sems, qa=tuple(qa), qbb=tuple(qbb), e=e):
                with h.If_eq(preg[e], 0):
                    for fn in qa:
                        fn(h, sems)
                with h.Else():
                    for fn in qbb:
                        fn(h, sems)
            self.q[e].append(wrap)
        for e in ENGINES:
            ha, hb = fk["hwm_a"][e], self.hwm[e]
            self.hwm[e] = {s: min(ha.get(s, 0), hb.get(s, 0))
                           for s in set(ha) | set(hb)}
        for b in self.bufs:
            wa, ra = fk["bufs_a"].get(id(b), (None, []))
            if b.w is None or (wa is not None and wa >= b.w):
                b.w = wa
            seen = set(b.r)
            for t in ra:
                if t not in seen:
                    b.r.append(t)

    def emit_into(self, block, sems):
        for eng in ENGINES:
            ops = self.q[eng]
            if not ops:
                continue
            def section(h, ops=tuple(ops)):
                for fn in ops:
                    fn(h, sems)
            getattr(block, eng)(section)


def cdiv(a, b):
    return (a + b - 1) // b


def host_constants():
    c = {}
    c["ident_bf"] = np.eye(128, dtype=ml_dtypes.bfloat16)
    c["ident_f32"] = np.eye(128, dtype=np.float32)
    kk, qq = np.meshgrid(np.arange(128), np.arange(128), indexing="ij")
    c["tri_kq"] = (kk <= qq).astype(ml_dtypes.bfloat16)
    inv_freq = 1.0 / (10000.0 ** (np.arange(0, 64, 2, dtype=np.float64) / 64.0))
    ang = np.arange(S, dtype=np.float64)[:, None] * inv_freq[None, :]
    c["cos_all"] = np.cos(ang).astype(np.float32)
    c["sin_all"] = np.sin(ang).astype(np.float32)
    c["flags"] = (FLAGBASE + np.arange(1, 65, dtype=np.int64)).astype(
        np.int32)[None, :]
    c["ones_f32"] = np.ones((1, 128), dtype=np.float32)
    return c


def host_bands(wT, sw):
    K, N = wT.shape
    c = K // 128
    a = np.asarray(wT).reshape(c, 128, N)
    bands = [np.ascontiguousarray(a[:, :, i:i + sw].transpose(1, 0, 2)
                                  ).reshape(128, c * sw)
             for i in range(0, N, sw)]
    return np.ascontiguousarray(np.stack(bands))


def host_bands_rows(wT, rows_per_band):
    K, N = wT.shape
    c = K // 128
    a = np.asarray(wT).reshape(c, 128, N)
    rb = rows_per_band
    bands = [np.ascontiguousarray(a[b * rb:(b + 1) * rb].transpose(1, 0, 2)
                                  ).reshape(128, rb * N)
             for b in range(c // rb)]
    return np.ascontiguousarray(np.stack(bands))


def own_rows(parity):
    out = []
    for pblk in OWNPOS[parity]:
        out.extend(range(pblk * BLKT, (pblk + 1) * BLKT))
    return np.array(out)


def causal_ktiles(pq):
    ds = GSLOT[pq]
    fulls = [j for j in range(16) if GPOS[j // 4] < pq]
    out = [(ds * 4, 0)]
    out += [(j, 0) for j in fulls]
    out += [(ds * 4 + r, r * 128) for r in (1, 2, 3)]
    return out


class BS:
    pass


def make_bs(blocks, dev_taps=()):
    nc = bass.Bass("TRN2", target_bir_lowering=False, num_devices=NCORES)
    p = Prog(nc)
    ctx = ExitStack()
    e = ctx.enter_context
    bs = BS()
    bs.nc, bs.p, bs.e, bs.ctx = nc, p, e, ctx
    bs.blocks = blocks

    bs.x_in = nc.dram_tensor("x_in", [TPC, D], F32, kind="ExternalInput").ap()
    bs.x_out = nc.dram_tensor("x_out", [TPC, D], F32,
                              kind="ExternalOutput").ap()
    bs.dram = {}

    def din(name, shape, dtype=F32):
        bs.dram[name] = nc.dram_tensor(name, shape, dtype,
                                       kind="ExternalInput").ap()

    for name, shape, dt in [
            ("ident_bf", [128, 128], BF16), ("ident_f32", [128, 128], F32),
            ("tri_kq", [128, 128], BF16),
            ("cos_own", [TPC, 32], BF16), ("sin_own", [TPC, 32], BF16),
            ("cos_own_fm", [64, TPC], BF16), ("sin_own_fm", [64, TPC], BF16),
            ("flags", [1, 64], I32), ("ones_f32", [1, 128], F32),
            ("parity", [1, 1], I32)]:
        din(name, shape, dt)
    if "l0attn" in blocks:
        din("wq_a", [12, 128, 16 * 128]); din("gq_a", [128, 12])
        din("wq_b_n", [H0, 128, 12 * 128]); din("wq_b_r", [H0, 128, 12 * 64])
        din("wkv_a", [1, 128, 16 * DKV]); din("gkv_a", [128, 4])
        din("wkv_b_k", [H0, 128, 4 * 128]); din("wkv_b_v", [H0, 128, 4 * 128])
        din("wo", [16, 128, 16 * 128])
    if "l0ffn" in blocks:
        din("gate_wT", [32, 128, 16 * 256], I32)
        din("up_wT", [32, 128, 16 * 256], I32)
        din("down_wT", [32, 128, 2 * D], I32)
        din("gate_scale", [128, 64]); din("up_scale", [128, 64])
        din("down_scale", [128, 16])
    if "l1attn" in blocks:
        din("wqkv_q", [H1, 128, 16 * 64]); din("wqkv_kv", [1, 128, 16 * 128])
        din("w_dense", [16, 128, 16 * 128])
    if "l1ffn" in blocks:
        din("WcT", [16, 128, 16 * 256]); din("genes", [2, 128, 32 * 128])
        din("WdT", [16, 128, 2 * 128]); din("WuT", [16, 128, 2 * 128])
        din("norm_w", [128, 16]); din("temperature", [128, 1])
        din("ffn_scale", [128, 1])

    bs.shared = {}
    if "l0attn" in blocks:
        bs.shared["kv0"] = nc.dram_tensor("sh_kv0", [2 * TPC, DKV], BF16,
                                          addr_space="Shared").ap()
    if "l1attn" in blocks:
        bs.shared["kv1"] = nc.dram_tensor("sh_kv1", [2 * TPC, 2 * HD1], BF16,
                                          addr_space="Shared").ap()
    if bs.shared:
        bs.shared["flags"] = nc.dram_tensor("sh_flags", [2, 64], I32,
                                            addr_space="Shared").ap()
    bs.taps = {}
    for name, shape, dt in dev_taps:
        bs.taps[name] = nc.dram_tensor("tap_" + name, shape, dt,
                                       kind="ExternalOutput").ap()

    bs.x_fm = e(nc.sbuf_tensor("x_fm", [128, 16 * TPC], F32))
    bs.xB = p.buf("x_fm")
    bs.arena = e(nc.sbuf_tensor("arena", [128, 16 * TPC], BF16))
    bs.arenaB = p.buf("arena")
    bs.cst = {}
    bs.cstB = p.buf("cst")
    for nm, shp, dt in [("ident_bf", [128, 128], BF16),
                        ("ident_f32", [128, 128], F32),
                        ("tri_kq", [128, 128], BF16),
                        ("cos_own", [128, 8 * 32], BF16),
                        ("sin_own", [128, 8 * 32], BF16),
                        ("cos_own_fm", [64, TPC], BF16),
                        ("sin_own_fm", [64, TPC], BF16),
                        ("flags", [1, 64], I32),
                        ("ones_f32", [1, 128], F32),
                        ("parity", [1, 1], I32)]:
        bs.cst[nm] = e(nc.sbuf_tensor("c_" + nm, shp, dt))
    bs.cst["ones_bf"] = e(nc.sbuf_tensor("c_ones_bf", [128, 8], BF16))
    bs.cst["zeros"] = e(nc.sbuf_tensor("c_zeros", [1, 64], I32))
    bs.spoll = e(nc.sbuf_tensor("spoll", [1, 1], I32))
    bs.fbank = e(nc.sbuf_tensor("fbank", [128, 8], F32))
    bs.fvals = {}

    def fc(v, rows=1):
        if v not in bs.fvals:
            i = len(bs.fvals)
            assert i < 8
            bs.fvals[v] = i
            p.op("vector", lambda h, s, i=i, v=v:
                 h.memset(bs.fbank[:, i:i + 1], v), writes=[bs.cstB])
        i = bs.fvals[v]
        return bs.fbank[0:rows, i:i + 1]
    bs.fc = fc
    bs.wband = e(nc.sbuf_tensor("wband", [128, 9216], BF16))
    bs.wbB = {}
    bs.pb = [e(nc.psum_tensor(f"pb{i}", [128, 512], F32)) for i in range(7)]
    bs.pbB = [p.buf(f"pb{i}", psum=True) for i in range(7)]
    bs.pbf = e(nc.psum_tensor("pbf", [128, 1024], BF16))
    bs.pbfB = p.buf("pbf", psum=True)
    bs.round = 0
    bs.preg = {}

    for nm in ["ident_bf", "ident_f32", "tri_kq", "flags", "ones_f32",
               "parity", "cos_own_fm", "sin_own_fm"]:
        p.dma("sync", bs.cst[nm][:], bs.dram[nm], writes=[bs.cstB])
    p.dma("sync", bs.cst["cos_own"][:].rearrange("p (c n) -> p c n", c=8),
          bs.dram["cos_own"].rearrange("(c p) n -> p c n", p=128),
          writes=[bs.cstB])
    p.dma("sync", bs.cst["sin_own"][:].rearrange("p (c n) -> p c n", c=8),
          bs.dram["sin_own"].rearrange("(c p) n -> p c n", p=128),
          writes=[bs.cstB])
    p.op("vector", lambda h, s: h.memset(bs.cst["ones_bf"][:], 1.0),
         writes=[bs.cstB])
    p.op("vector", lambda h, s: h.memset(bs.cst["zeros"][:], 0),
         writes=[bs.cstB])
    ctok = bs.cstB.w
    for eng in ENGINES:
        def initreg(h, sems, eng=eng, ctok=ctok):
            h.wait_ge(sems[ctok[0]], ctok[1])
            r = h.alloc_register(f"preg_{eng}")
            bs.preg[eng] = r
            h.reg_load(r, bs.cst["parity"][0:1, 0:1])
        p.raw(eng, initreg)
    return bs


def wslot(bs, a, b):
    key = (a, b)
    if key not in bs.wbB:
        bs.wbB[key] = bs.p.buf(f"wb{a}_{b}")
    bufs = [bf for (x, y), bf in bs.wbB.items() if x < b and y > a]
    return bs.wband[:, a:b], bufs


def finish(bs):
    p, nc = bs.p, bs.nc
    finals = [(f"dsy{i}", p.cnt[f"dsy{i}"]) for i in range(NLANES)]
    def endwait(h, sems, finals=tuple(finals)):
        for s, v in finals:
            if v > 0:
                h.wait_ge(sems[s], v)
    p.raw("sync", endwait)
    with ExitStack() as ectx:
        sems = {}
        for snm in p.semnames:
            sems[snm] = ectx.enter_context(nc.semaphore(snm))
        block = ectx.enter_context(nc.Block())
        p.emit_into(block, sems)
    bs.ctx.close()
    return nc



def band_dma(p, eng, dst_sl, dram3, b, writes, reads=()):
    return p.dma(eng, dst_sl, dram3[b, :, :], writes=writes, reads=reads)


def accum_band_dma(p, dst_sl, dram3, b, writes):
    C = dram3.shape[-1]
    tok = None
    for o in range(0, C, 2048):
        e = min(o + 2048, C)
        tok = p.dma("gpsimd", dst_sl[:, o:e], dram3[b, :, o:e],
                    writes=writes, accum_op=ALU.add)
    return tok


def emit_load_x(bs):
    nc, p = bs.nc, bs.p
    with nc.sbuf_tensor("xstage", [128, 2 * D], F32) as xstage:
        stg = [p.buf("xs0"), p.buf("xs1")]
        for tt in range(8):
            sl = xstage[:, (tt % 2) * D:(tt % 2) * D + D]
            p.dma("sync", sl, bs.x_in[tt * 128:(tt + 1) * 128, :],
                  writes=[stg[tt % 2]])
            for f in range(16):
                bk, bkB = bs.pb[f % 2], bs.pbB[f % 2]
                src = sl[:, f * 128:(f + 1) * 128]
                dst = bs.x_fm[:, f * TPC + tt * 128: f * TPC + tt * 128 + 128]
                p.op("tensor", lambda h, s, bk=bk, src=src:
                     h.transpose(bk[:, 0:128], src, bs.cst["ident_f32"][:]),
                     reads=[stg[tt % 2], bs.cstB], writes=[bkB])
                p.op("vector", lambda h, s, bk=bk, dst=dst:
                     h.tensor_copy(out=dst, in_=bk[:, 0:128]),
                     reads=[bkB], writes=[bs.xB])
        p.barrier(stg)


def emit_store_x(bs):
    nc, p = bs.nc, bs.p
    with nc.sbuf_tensor("ostage", [128, 2 * D], F32) as ostage:
        stg = [p.buf("os0"), p.buf("os1")]
        for tt in range(8):
            sl = ostage[:, (tt % 2) * D:(tt % 2) * D + D]
            for f in range(16):
                bk, bkB = bs.pb[f % 2], bs.pbB[f % 2]
                src = bs.x_fm[:, f * TPC + tt * 128: f * TPC + tt * 128 + 128]
                p.op("tensor", lambda h, s, bk=bk, src=src:
                     h.transpose(bk[:, 0:128], src, bs.cst["ident_f32"][:]),
                     reads=[bs.xB, bs.cstB], writes=[bkB])
                p.op("vector", lambda h, s, bk=bk, sl=sl, f=f:
                     h.tensor_copy(out=sl[:, f * 128:(f + 1) * 128],
                                   in_=bk[:, 0:128]),
                     reads=[bkB], writes=[stg[tt % 2]])
            p.dma("sync", bs.x_out[tt * 128:(tt + 1) * 128, :], sl,
                  reads=[stg[tt % 2]])
        p.barrier(stg)


def emit_tap(bs, name, sbuf_ap, readB):
    if name in bs.taps:
        bs.p.dma("sync", bs.taps[name], sbuf_ap, reads=[readB])


def emit_ln_fm(bs, out_arena, outB, eps=EPS_LN, kind="ln", nfeat=16,
               src=None, srcB=None, gamma=None, gammaB=None):
    nc, p = bs.nc, bs.p
    if src is None:
        src, srcB = bs.x_fm, bs.xB
    DT = nfeat * 128
    pS, pSB = bs.pb[4], bs.pbB[4]
    pQ, pQB = bs.pb[5], bs.pbB[5]
    pBC, pBCB = bs.pb[6], bs.pbB[6]
    with nc.sbuf_tensor("lnsc", [128, 2 * 512], BF16) as xb, \
         nc.sbuf_tensor("lnsq", [128, 2 * 512], BF16) as sq, \
         nc.sbuf_tensor("lnrow", [1, 2 * TPC], F32) as rows, \
         nc.sbuf_tensor("lnbc", [128, 2 * 512], F32) as bc:
        xbB = [p.buf("xb0"), p.buf("xb1")]
        sqB = [p.buf("sq0"), p.buf("sq1")]
        rowsB = p.buf("rows")
        bcB = p.buf("bc")
        for tr in range(2):
            for f in range(nfeat):
                xsl = src[:, f * TPC + tr * 512: f * TPC + tr * 512 + 512]
                xbs = xb[:, (f % 2) * 512:(f % 2) * 512 + 512]
                sqs = sq[:, (f % 2) * 512:(f % 2) * 512 + 512]
                p.op("vector", lambda h, s, xsl=xsl, xbs=xbs:
                     h.tensor_copy(out=xbs, in_=xsl),
                     reads=[srcB], writes=[xbB[f % 2]])
                if kind == "ln":
                    p.op("tensor", lambda h, s, xbs=xbs, f=f:
                         h.matmul(pS[0:1, :], bs.cst["ones_bf"][:, 0:1], xbs,
                                  start=(f == 0), stop=(f == nfeat - 1)),
                         reads=[xbB[f % 2], bs.cstB], writes=[pSB])
                p.op("scalar", lambda h, s, xbs=xbs, sqs=sqs:
                     h.activation(out=sqs, in_=xbs, func=AF.Square),
                     reads=[xbB[f % 2]], writes=[sqB[f % 2]])
                p.op("tensor", lambda h, s, sqs=sqs, f=f:
                     h.matmul(pQ[0:1, :], bs.cst["ones_bf"][:, 0:1], sqs,
                              start=(f == 0), stop=(f == nfeat - 1)),
                     reads=[sqB[f % 2], bs.cstB], writes=[pQB])
            ro = rows[:, tr * 512:(tr + 1) * 512]
            rm = rows[:, TPC + tr * 512:TPC + tr * 512 + 512]
            if kind == "ln":
                p.op("vector", lambda h, s, rm=rm:
                     h.tensor_scalar(out=rm, in0=pS[0:1, :], scalar1=1.0 / DT,
                                     scalar2=None, op0=ALU.mult),
                     reads=[pSB], writes=[rowsB])
                p.op("vector", lambda h, s, ro=ro:
                     h.tensor_scalar(out=ro, in0=pQ[0:1, :], scalar1=1.0 / DT,
                                     scalar2=None, op0=ALU.mult),
                     reads=[pQB], writes=[rowsB])
                p.op("vector", lambda h, s, rm=rm:
                     h.tensor_tensor(out=bc[0:1, 0:512], in0=rm, in1=rm,
                                     op=ALU.mult),
                     reads=[rowsB], writes=[bcB])
                p.op("vector", lambda h, s, ro=ro:
                     h.tensor_tensor(out=ro, in0=ro, in1=bc[0:1, 0:512],
                                     op=ALU.subtract),
                     reads=[rowsB, bcB], writes=[rowsB])
            else:
                p.op("vector", lambda h, s, ro=ro:
                     h.tensor_scalar(out=ro, in0=pQ[0:1, :], scalar1=1.0 / DT,
                                     scalar2=None, op0=ALU.mult),
                     reads=[pQB], writes=[rowsB])
            eps_ap = bs.fc(eps)
            p.op("scalar", lambda h, s, ro=ro, eps_ap=eps_ap:
                 h.activation(out=ro, in_=ro, func=AF.Ln, bias=eps_ap),
                 reads=[rowsB, bs.cstB], writes=[rowsB])
            p.op("scalar", lambda h, s, ro=ro:
                 h.activation(out=ro, in_=ro, func=AF.Exp, scale=-0.5),
                 reads=[rowsB], writes=[rowsB])
            if kind == "ln":
                p.op("tensor", lambda h, s, rm=rm:
                     h.matmul(pBC[:, 0:512], bs.cst["ones_f32"][0:1, :], rm,
                              start=True, stop=True),
                     reads=[rowsB, bs.cstB], writes=[pBCB])
                p.op("vector", lambda h, s:
                     h.tensor_copy(out=bc[:, 0:512], in_=pBC[:, 0:512]),
                     reads=[pBCB], writes=[bcB])
            p.op("tensor", lambda h, s, ro=ro:
                 h.matmul(pBC[:, 0:512], bs.cst["ones_f32"][0:1, :], ro,
                          start=True, stop=True),
                 reads=[rowsB, bs.cstB], writes=[pBCB])
            p.op("vector", lambda h, s:
                 h.tensor_copy(out=bc[:, 512:1024], in_=pBC[:, 0:512]),
                 reads=[pBCB], writes=[bcB])
            for f in range(nfeat):
                xsl = src[:, f * TPC + tr * 512: f * TPC + tr * 512 + 512]
                osl = out_arena[:, f * TPC + tr * 512: f * TPC + tr * 512 + 512]
                if kind == "ln":
                    p.op("vector", lambda h, s, xsl=xsl, osl=osl:
                         h.tensor_tensor(out=osl, in0=xsl, in1=bc[:, 0:512],
                                         op=ALU.subtract),
                         reads=[srcB, bcB], writes=[outB])
                    p.op("vector", lambda h, s, osl=osl:
                         h.tensor_tensor(out=osl, in0=osl, in1=bc[:, 512:1024],
                                         op=ALU.mult),
                         reads=[bcB], writes=[outB])
                else:
                    p.op("vector", lambda h, s, xsl=xsl, osl=osl:
                         h.tensor_tensor(out=osl, in0=xsl, in1=bc[:, 512:1024],
                                         op=ALU.mult),
                         reads=[srcB, bcB], writes=[outB])
                if gamma is not None:
                    p.op("vector", lambda h, s, osl=osl, f=f:
                         h.tensor_scalar(out=osl, in0=osl,
                                         scalar1=gamma[:, f:f + 1],
                                         scalar2=None, op0=ALU.mult),
                         params=[gammaB], writes=[outB])
        p.barrier([xbB[0], xbB[1], sqB[0], sqB[1], rowsB, bcB])


def tok_buf(tok, name="tok"):
    b = Buf(name)
    b.w = tok
    return b


def emit_rendezvous(bs):
    nc, p = bs.nc, bs.p
    rnd = bs.round
    bs.round += 1
    fl = bs.shared["flags"]
    expect = FLAGBASE + rnd + 1
    data_toks = [(f"dsy{i}", p.cnt[f"dsy{i}"]) for i in range(NLANES)]
    p.fork()
    for branch_par in (0, 1):
        if branch_par == 1:
            p.alt()
        par = branch_par
        p.dma("sync", fl[par:par + 1, rnd:rnd + 1],
              bs.cst["flags"][0:1, rnd:rnd + 1],
              reads=[bs.cstB] + [tok_buf(t, "xdata") for t in data_toks])

        def pollfn(h, sems, expect=expect, rnd=rnd, par=par):
            with h.register(f"c{rnd}_{par}") as c, \
                 h.register(f"f{rnd}_{par}") as f:
                h.reg_mov(c, 1)
                with h.While(c):
                    h.sem_clear(sems["rdv"])
                    h.dma_start(out=bs.spoll[:],
                                in_=fl[1 - par:2 - par, rnd:rnd + 1]
                                ).then_inc(sems["rdv"], 16)
                    h.wait_ge(sems["rdv"], 16)
                    h.reg_load(f, bs.spoll[0:1, 0:1])
                    with h.If_eq(f, expect):
                        h.reg_mov(c, 0)
        p.raw("sync", pollfn)
        p.dma("sync", fl[1 - par:2 - par, rnd:rnd + 1],
              bs.cst["zeros"][0:1, 0:1], reads=[bs.cstB])
    p.join(bs.preg)
    tok = p.op("sync", lambda h, s: h.nop())
    return tok


def emit_l0attn(bs):
    nc, p = bs.nc, bs.p
    LIM = int(os.environ.get("L0LIM", "99"))
    NHEADS = int(os.environ.get("L0HEADS", str(H0)))
    GLIM0 = int(os.environ.get("L0GLIM", "99"))
    GLIM1 = int(os.environ.get("L0GLIM1", "99"))
    emit_ln_fm(bs, bs.arena, bs.arenaB, eps=EPS_LN, kind="ln")
    arena, arenaB = bs.arena, bs.arenaB
    if LIM <= 0:
        return
    dram = bs.dram

    with nc.sbuf_tensor("qlr", [128, 12 * TPC], BF16) as qlr, \
         nc.sbuf_tensor("ckv_fm", [128, 4 * 2048], BF16) as ckv_fm, \
         nc.sbuf_tensor("krope_fm", [64, 2048], BF16) as krope_fm, \
         nc.sbuf_tensor("kvtok", [128, 2 * DKV], BF16) as kvtok, \
         nc.sbuf_tensor("l0rows", [1, 576], F32) as rows, \
         nc.sbuf_tensor("krsq2", [64, 2048], BF16) as krsq2, \
         nc.sbuf_tensor("l0r2", [128, 8], F32) as r2, \
         nc.sbuf_tensor("vg", [128, 16 * 512], BF16) as vg, \
         nc.sbuf_tensor("kfm", [128, 2048], BF16) as kfm, \
         nc.sbuf_tensor("sqh", [128, 1024], BF16) as sqh, \
         nc.sbuf_tensor("qh", [128, TPC], BF16) as qh, \
         nc.sbuf_tensor("qrh", [64, 2 * TPC], BF16) as qrh, \
         nc.sbuf_tensor("probs", [128, 2 * 512], BF16) as probs, \
         nc.sbuf_tensor("stg", [128, DKV], BF16) as stg, \
         nc.sbuf_tensor("rzt", [128, 513], F32) as rzt, \
         nc.sbuf_tensor("gqv", [128, 12 + 4], F32) as gqv:

        qlrB = p.buf("qlr"); ckvB = p.buf("ckv"); kroB = p.buf("kro")
        kvtB = [p.buf("kvt0"), p.buf("kvt1")]
        rowsB = p.buf("rows0"); r2B = p.buf("r2")
        vgB = p.buf("vg"); kfmB = p.buf("kfm"); sqhB = p.buf("sqh")
        qhB = p.buf("qh"); qrhB = p.buf("qrh")
        prB = [p.buf("pr0"), p.buf("pr1")]
        krsq2B = p.buf("krsq2")
        stgB = [p.buf("stg0")]
        rztB = p.buf("rzt")
        gqvB = p.buf("gqv")

        p.dma("sync", gqv[:, 0:12], dram["gq_a"], writes=[gqvB])
        p.dma("sync", gqv[:, 12:16], dram["gkv_a"], writes=[gqvB])

        for m in range(12):
            a0 = (m % 2) * 2048
            wsl, wbufs = wslot(bs, a0, a0 + 2048)
            band_dma(p, "gpsimd", wsl[:, 0:2048], dram["wq_a"], m, wbufs)
            wB = bs.wbB[(a0, a0 + 2048)]
            for tr in range(2):
                pq, pqB = bs.pb[tr], bs.pbB[tr]
                for f in range(16):
                    lhs = wsl[:, f * 128:(f + 1) * 128]
                    rhs = arena[:, f * TPC + tr * 512: f * TPC + tr * 512 + 512]
                    p.op("tensor", lambda h, s, lhs=lhs, rhs=rhs, pq=pq, f=f:
                         h.matmul(pq[:], lhs, rhs, start=(f == 0),
                                  stop=(f == 15)),
                         reads=[wB, arenaB], writes=[pqB])
                p.op("vector", lambda h, s, pq=pq, m=m, tr=tr:
                     h.tensor_copy(out=qlr[:, m * TPC + tr * 512:
                                           m * TPC + tr * 512 + 512],
                                   in_=pq[:]),
                     reads=[pqB], writes=[qlrB])
        pQ, pQB = bs.pb[4], bs.pbB[4]
        pBC, pBCB = bs.pb[5], bs.pbB[5]
        for tr in range(2):
            for m in range(12):
                qsl = qlr[:, m * TPC + tr * 512: m * TPC + tr * 512 + 512]
                sqs = sqh[:, (m % 2) * 512:(m % 2) * 512 + 512]
                p.op("scalar", lambda h, s, qsl=qsl, sqs=sqs:
                     h.activation(out=sqs, in_=qsl, func=AF.Square),
                     reads=[qlrB], writes=[sqhB])
                p.op("tensor", lambda h, s, sqs=sqs, m=m:
                     h.matmul(pQ[0:1, :], bs.cst["ones_bf"][:, 0:1], sqs,
                              start=(m == 0), stop=(m == 11)),
                     reads=[sqhB, bs.cstB], writes=[pQB])
            ro = rows[:, 0:512]
            p.op("vector", lambda h, s, ro=ro:
                 h.tensor_scalar(out=ro, in0=pQ[0:1, :], scalar1=1.0 / QLR,
                                 scalar2=None, op0=ALU.mult),
                 reads=[pQB], writes=[rowsB])
            eps0_ap = bs.fc(EPS_RMS0)
            p.op("scalar", lambda h, s, ro=ro, eps0_ap=eps0_ap:
                 h.activation(out=ro, in_=ro, func=AF.Ln, bias=eps0_ap),
                 reads=[rowsB, bs.cstB], writes=[rowsB])
            p.op("scalar", lambda h, s, ro=ro:
                 h.activation(out=ro, in_=ro, func=AF.Exp, scale=-0.5),
                 reads=[rowsB], writes=[rowsB])
            p.op("tensor", lambda h, s, ro=ro:
                 h.matmul(pBC[:, 0:512], bs.cst["ones_f32"][0:1, :], ro,
                          start=True, stop=True),
                 reads=[rowsB, bs.cstB], writes=[pBCB])
            p.op("vector", lambda h, s, tr=tr:
                 h.tensor_copy(out=probs[:, tr * 512:tr * 512 + 512],
                               in_=pBC[:, 0:512]),
                 reads=[pBCB], writes=[prB[0]])
            for m in range(12):
                sl = qlr[:, m * TPC + tr * 512: m * TPC + tr * 512 + 512]
                p.op("vector", lambda h, s, sl=sl, tr=tr:
                     h.tensor_tensor(out=sl, in0=sl,
                                     in1=probs[:, tr * 512:tr * 512 + 512],
                                     op=ALU.mult),
                     reads=[prB[0]], writes=[qlrB])
                p.op("vector", lambda h, s, sl=sl, m=m:
                     h.tensor_scalar(out=sl, in0=sl,
                                     scalar1=gqv[:, m:m + 1],
                                     scalar2=None, op0=ALU.mult),
                     params=[gqvB], writes=[qlrB])
        emit_tap(bs, "qlr", qlr[:], qlrB)
        if LIM <= 1:
            return

        wkv_sl, wkv_bufs = wslot(bs, 0, 9216)
        band_dma(p, "gpsimd", wkv_sl, dram["wkv_a"], 0, wkv_bufs)
        wkvB = bs.wbB[(0, 9216)]
        for tt in range(8):
            pa, paB = bs.pb[tt % 2], bs.pbB[tt % 2]
            pbb, pbbB = bs.pb[2 + tt % 2], bs.pbB[2 + tt % 2]
            for f in range(16):
                lhs = arena[:, f * TPC + tt * 128: f * TPC + tt * 128 + 128]
                p.op("tensor", lambda h, s, lhs=lhs, f=f, pa=pa:
                     h.matmul(pa[:], lhs, wkv_sl[:, f * 576:f * 576 + 512],
                              start=(f == 0), stop=(f == 15)),
                     reads=[wkvB, arenaB], writes=[paB])
                p.op("tensor", lambda h, s, lhs=lhs, f=f, pbb=pbb:
                     h.matmul(pbb[:, 0:64], lhs,
                              wkv_sl[:, f * 576 + 512:f * 576 + 576],
                              start=(f == 0), stop=(f == 15)),
                     reads=[wkvB, arenaB], writes=[pbbB])
            dst = kvtok[:, (tt % 2) * DKV: (tt % 2) * DKV + DKV]
            dstB = kvtB[tt % 2]
            p.op("scalar", lambda h, s, pa=pa:
                 h.activation(out=sqh[:, 0:512], in_=pa[:], func=AF.Square,
                              accum_out=r2[:, 0:1]),
                 reads=[paB], writes=[sqhB, r2B])
            eps0128 = bs.fc(EPS_RMS0, 128)
            p.op("scalar", lambda h, s, eps0128=eps0128:
                 h.activation(out=r2[:, 1:2], in_=r2[:, 0:1], func=AF.Ln,
                              bias=eps0128, scale=1.0 / KVLR),
                 reads=[r2B, bs.cstB], writes=[r2B])
            p.op("scalar", lambda h, s:
                 h.activation(out=r2[:, 1:2], in_=r2[:, 1:2], func=AF.Exp,
                              scale=-0.5),
                 reads=[r2B], writes=[r2B])
            p.op("vector", lambda h, s, pa=pa, dst=dst:
                 h.tensor_scalar(out=dst[:, 0:KVLR], in0=pa[:],
                                 scalar1=r2[:, 1:2], scalar2=None,
                                 op0=ALU.mult),
                 reads=[paB], params=[r2B], writes=[dstB])
            cosl = bs.cst["cos_own"][:, tt * 32:(tt + 1) * 32]
            sinl = bs.cst["sin_own"][:, tt * 32:(tt + 1) * 32]
            krbf = probs[:, 128:192]
            p.op("vector", lambda h, s, pbb=pbb, krbf=krbf:
                 h.tensor_copy(out=krbf, in_=pbb[:, 0:64]),
                 reads=[pbbB], writes=[prB[0]])
            t4 = probs[:, 0:128]
            for i, (tab, c0) in enumerate([(cosl, 0), (sinl, 32),
                                           (sinl, 0), (cosl, 32)]):
                p.op("vector", lambda h, s, krbf=krbf, tab=tab, c0=c0, i=i:
                     h.tensor_tensor(out=t4[:, i * 32:(i + 1) * 32],
                                     in0=krbf[:, c0:c0 + 32],
                                     in1=tab, op=ALU.mult),
                     reads=[prB[0], bs.cstB], writes=[prB[0]])
            p.op("vector", lambda h, s, dst=dst, t4=t4:
                 h.tensor_tensor(out=dst[:, KVLR:KVLR + 32], in0=t4[:, 0:32],
                                 in1=t4[:, 32:64], op=ALU.subtract),
                 reads=[prB[0]], writes=[dstB])
            p.op("vector", lambda h, s, dst=dst, t4=t4:
                 h.tensor_tensor(out=dst[:, KVLR + 32:KVLR + 64],
                                 in0=t4[:, 64:96], in1=t4[:, 96:128],
                                 op=ALU.add),
                 reads=[prB[0]], writes=[dstB])
            shkv = bs.shared["kv0"]
            p.fork()
            for par_ in (0, 1):
                if par_ == 1:
                    p.alt()
                sdst = shkv[par_ * TPC + tt * 128:
                            par_ * TPC + tt * 128 + 128, :]
                p.dma("sync", sdst, dst, reads=[dstB])
            p.join(bs.preg)
        if LIM <= 2:
            return
        rtok = emit_rendezvous(bs)
        rdvB = tok_buf(rtok, "rdv0")
        sh = bs.shared["kv0"]
        sh = bs.shared["kv0"]
        for j in range(16):
            ssl = stg[:, 0:DKV]
            p.dma("sync", ssl, sh[j * 128:(j + 1) * 128, :],
                  reads=[rdvB], writes=[stgB[0]])
            for c in range(4):
                p.op("tensor", lambda h, s, ssl=ssl, c=c:
                     h.transpose(bs.pbf[:, 0:128],
                                 ssl[:, c * 128:(c + 1) * 128],
                                 bs.cst["ident_bf"][:]),
                     reads=[stgB[0], bs.cstB], writes=[bs.pbfB])
                p.op("vector", lambda h, s, c=c, j=j:
                     h.tensor_scalar(out=ckv_fm[:, c * 2048 + j * 128:
                                                c * 2048 + j * 128 + 128],
                                     in0=bs.pbf[:, 0:128],
                                     scalar1=gqv[:, 12 + c:13 + c],
                                     scalar2=None, op0=ALU.mult),
                     reads=[bs.pbfB], params=[gqvB], writes=[ckvB])
            p.op("tensor", lambda h, s, ssl=ssl:
                 h.transpose(bs.pbf[0:64, 128:256], ssl[:, KVLR:KVLR + 64],
                             bs.cst["ident_bf"][:]),
                 reads=[stgB[0], bs.cstB], writes=[bs.pbfB])
            p.op("vector", lambda h, s, j=j:
                 h.tensor_copy(out=krope_fm[:, j * 128:j * 128 + 128],
                               in_=bs.pbf[0:64, 128:256]),
                 reads=[bs.pbfB], writes=[kroB])
        emit_tap(bs, "ckv_fm", ckv_fm[:], ckvB)

        p.op("scalar", lambda h, s:
             h.activation(out=krsq2[:], in_=krope_fm[:], func=AF.Square),
             reads=[kroB], writes=[krsq2B])

        if LIM <= 3:
            return
        for hh in range(NHEADS):
            GLIM = GLIM0 if hh == 0 else GLIM1
            g4 = hh % 4
            if g4 == 0:
                vb, vb_bufs = wslot(bs, 6144, 8192)
                vbB = bs.wbB[(6144, 8192)]
                for ii in range(4):
                    band_dma(p, "gpsimd", vb[:, ii * 512:(ii + 1) * 512],
                             dram["wkv_b_v"], hh + ii, vb_bufs)
                for ii in range(4):
                    for tt in range(16):
                        pv, pvB = bs.pb[tt % 2], bs.pbB[tt % 2]
                        for c in range(4):
                            lhs = ckv_fm[:, c * 2048 + tt * 128:
                                         c * 2048 + tt * 128 + 128]
                            rhsv = vb[:, ii * 512 + c * 128:
                                      ii * 512 + (c + 1) * 128]
                            p.op("tensor", lambda h, s, lhs=lhs, rhsv=rhsv,
                                 c=c, pv=pv:
                                 h.matmul(pv[:, 0:128], lhs, rhsv,
                                          start=(c == 0), stop=(c == 3)),
                                 reads=[vbB, ckvB], writes=[pvB])
                        p.op("vector", lambda h, s, pv=pv, tt=tt, ii=ii:
                             h.tensor_copy(
                                 out=vg[:, ii * 2048 + tt * 128:
                                        ii * 2048 + (tt + 1) * 128],
                                 in_=pv[:, 0:128]),
                             reads=[pvB], writes=[vgB])
            if GLIM <= 1:
                continue
            G2S = int(os.environ.get("L0G2SUB", "99")) if hh > 0 else \
                int(os.environ.get("L0G2ALL", "99"))
            a0 = 8192 + (hh % 2) * 512
            kb, kb_bufs = wslot(bs, a0, a0 + 512)
            kbB = bs.wbB[(a0, a0 + 512)]
            hh_src = int(os.environ.get("KSRC", "-1"))
            hh_eff = hh_src if (hh_src >= 0 and hh > 0) else hh
            band_dma(p, "gpsimd", kb, dram["wkv_b_k"], hh_eff, kb_bufs)
            if G2S <= 1:
                continue
            for n in range(4):
                kbank = int(os.environ.get("KBANK", "2"))
                pk, pkB = bs.pb[kbank + n % 2], bs.pbB[kbank + n % 2]
                for c in range(4):
                    rhs = ckv_fm[:, c * 2048 + n * 512:c * 2048 + n * 512 + 512]
                    p.op("tensor", lambda h, s, rhs=rhs, c=c, pk=pk, kb=kb:
                         h.matmul(pk[:], kb[:, c * 128:(c + 1) * 128], rhs,
                                  start=(c == 0), stop=(c == 3)),
                         reads=[kbB, ckvB], writes=[pkB])
                if G2S <= 2:
                    continue
                p.op("scalar", lambda h, s, pk=pk, n=n:
                     h.activation(out=sqh[:, (n % 2) * 512:(n % 2) * 512 + 512],
                                  in_=pk[:], func=AF.Square),
                     reads=[pkB], writes=[sqhB])
                if G2S <= 3 or (os.environ.get("L0NOEVAC") and hh > 0):
                    continue
                p.op("vector", lambda h, s, pk=pk, n=n:
                     h.tensor_copy(out=kfm[:, n * 512:(n + 1) * 512],
                                   in_=pk[:]),
                     reads=[pkB], writes=[kfmB])
                if G2S <= 4:
                    continue
                p.op("tensor", lambda h, s, n=n:
                     h.matmul(bs.pb[4][0:1, :], bs.cst["ones_bf"][:, 0:1],
                              sqh[:, (n % 2) * 512:(n % 2) * 512 + 512],
                              start=True, stop=False),
                     reads=[sqhB, bs.cstB], writes=[bs.pbB[4]])
                p.op("tensor", lambda h, s, n=n:
                     h.matmul(bs.pb[4][0:1, :], bs.cst["ones_bf"][0:64, 0:1],
                              krsq2[:, n * 512:(n + 1) * 512],
                              start=False, stop=True),
                     reads=[krsq2B, bs.cstB], writes=[bs.pbB[4]])
                p.op("vector", lambda h, s, n=n:
                     h.tensor_reduce(out=rows[:, 512 + n:513 + n],
                                     in_=bs.pb[4][0:1, :], axis=AXX,
                                     op=ALU.max),
                     reads=[bs.pbB[4]], writes=[rowsB])
            p.op("vector", lambda h, s:
                 h.tensor_reduce(out=rows[:, 516:517], in_=rows[:, 512:516],
                                 axis=AXX, op=ALU.max),
                 reads=[rowsB], writes=[rowsB])

            if GLIM <= 2:
                continue
            a1 = (hh % 2) * 2048
            qb, qb_bufs = wslot(bs, a1, a1 + 1536)
            qbB = bs.wbB[(a1, a1 + 1536)]
            band_dma(p, "gpsimd", qb, dram["wq_b_n"], hh, qb_bufs)
            a2 = 4096 + (hh % 2) * 1024
            qrb, qrb_bufs = wslot(bs, a2, a2 + 768)
            qrbB = bs.wbB[(a2, a2 + 768)]
            band_dma(p, "gpsimd", qrb, dram["wq_b_r"], hh, qrb_bufs)
            for tr in range(2):
                pn, pnB = bs.pb[tr], bs.pbB[tr]
                pr_, prB_ = bs.pb[2 + tr], bs.pbB[2 + tr]
                for c in range(12):
                    rhs = qlr[:, c * TPC + tr * 512: c * TPC + tr * 512 + 512]
                    p.op("tensor", lambda h, s, rhs=rhs, c=c, pn=pn, qb=qb:
                         h.matmul(pn[:], qb[:, c * 128:(c + 1) * 128], rhs,
                                  start=(c == 0), stop=(c == 11)),
                         reads=[qbB, qlrB], writes=[pnB])
                    p.op("tensor", lambda h, s, rhs=rhs, c=c, pr_=pr_,
                         qrb=qrb:
                         h.matmul(pr_[0:64, :], qrb[:, c * 64:(c + 1) * 64],
                                  rhs, start=(c == 0), stop=(c == 11)),
                         reads=[qrbB, qlrB], writes=[prB_])
                p.op("vector", lambda h, s, pn=pn, tr=tr:
                     h.tensor_copy(out=qh[:, tr * 512:(tr + 1) * 512],
                                   in_=pn[:]),
                     reads=[pnB], writes=[qhB])
                p.op("vector", lambda h, s, pr_=pr_, tr=tr:
                     h.tensor_copy(out=qrh[:, tr * 512:tr * 512 + 512],
                                   in_=pr_[0:64, :]),
                     reads=[prB_], writes=[qrhB])
                p.op("scalar", lambda h, s, pn=pn:
                     h.activation(out=sqh[:, 0:512], in_=pn[:],
                                  func=AF.Square),
                     reads=[pnB], writes=[sqhB])
                p.op("scalar", lambda h, s, pr_=pr_:
                     h.activation(out=sqh[0:64, 512:1024], in_=pr_[0:64, :],
                                  func=AF.Square),
                     reads=[prB_], writes=[sqhB])
                p.op("tensor", lambda h, s:
                     h.matmul(bs.pb[4][0:1, :], bs.cst["ones_bf"][:, 0:1],
                              sqh[:, 0:512], start=True, stop=False),
                     reads=[sqhB, bs.cstB], writes=[bs.pbB[4]])
                p.op("tensor", lambda h, s:
                     h.matmul(bs.pb[4][0:1, :], bs.cst["ones_bf"][0:64, 0:1],
                              sqh[0:64, 512:1024], start=False, stop=True),
                     reads=[sqhB, bs.cstB], writes=[bs.pbB[4]])
                p.op("vector", lambda h, s, tr=tr:
                     h.tensor_reduce(out=rows[:, 520 + tr:521 + tr],
                                     in_=bs.pb[4][0:1, :], axis=AXX,
                                     op=ALU.max),
                     reads=[bs.pbB[4]], writes=[rowsB])
            if GLIM <= 3:
                continue
            for tr in range(2):
                cosl = bs.cst["cos_own_fm"][:, tr * 512:(tr + 1) * 512]
                sinl = bs.cst["sin_own_fm"][:, tr * 512:(tr + 1) * 512]
                raw = qrh[0:64, tr * 512:tr * 512 + 512]
                o1 = qrh[0:32, TPC + tr * 512:TPC + tr * 512 + 512]
                o2 = qrh[32:64, TPC + tr * 512:TPC + tr * 512 + 512]
                ta = sqh[0:64, 0:512]
                tbs = sqh[0:64, 512:1024]
                p.op("vector", lambda h, s, raw=raw, ta=ta, cosl=cosl:
                     h.tensor_tensor(out=ta, in0=raw, in1=cosl, op=ALU.mult),
                     reads=[qrhB, bs.cstB], writes=[sqhB])
                p.op("vector", lambda h, s, raw=raw, tbs=tbs, sinl=sinl:
                     h.tensor_tensor(out=tbs[0:32, :], in0=raw[32:64, :],
                                     in1=sinl[32:64, :], op=ALU.mult),
                     reads=[qrhB, bs.cstB], writes=[sqhB])
                p.op("vector", lambda h, s, raw=raw, tbs=tbs, sinl=sinl:
                     h.tensor_tensor(out=tbs[32:64, :], in0=raw[0:32, :],
                                     in1=sinl[0:32, :], op=ALU.mult),
                     reads=[qrhB, bs.cstB], writes=[sqhB])
                p.op("vector", lambda h, s, o1=o1, ta=ta, tbs=tbs:
                     h.tensor_tensor(out=o1, in0=ta[0:32, :], in1=tbs[0:32, :],
                                     op=ALU.subtract),
                     reads=[sqhB], writes=[qrhB])
                p.op("vector", lambda h, s, o2=o2, ta=ta, tbs=tbs:
                     h.tensor_tensor(out=o2, in0=ta[32:64, :],
                                     in1=tbs[32:64, :], op=ALU.add),
                     reads=[sqhB], writes=[qrhB])
            if GLIM <= 4:
                continue
            p.op("vector", lambda h, s:
                 h.tensor_reduce(out=rows[:, 522:523],
                                 in_=rows[:, 520:522], axis=AXX, op=ALU.max),
                 reads=[rowsB], writes=[rowsB])
            p.op("scalar", lambda h, s:
                 h.activation(out=rows[:, 523:524], in_=rows[:, 516:517],
                              func=AF.Ln),
                 reads=[rowsB], writes=[rowsB])
            p.op("scalar", lambda h, s:
                 h.activation(out=rows[:, 524:525], in_=rows[:, 522:523],
                              func=AF.Ln),
                 reads=[rowsB], writes=[rowsB])
            p.op("vector", lambda h, s:
                 h.tensor_tensor(out=rows[:, 525:526],
                                 in0=rows[:, 523:524],
                                 in1=rows[:, 524:525], op=ALU.add),
                 reads=[rowsB], writes=[rowsB])
            p.op("scalar", lambda h, s:
                 h.activation(out=rows[:, 525:526], in_=rows[:, 525:526],
                              func=AF.Exp, scale=0.5),
                 reads=[rowsB], writes=[rowsB])
            p.op("vector", lambda h, s:
                 h.tensor_scalar(out=rows[:, 525:526],
                                 in0=rows[:, 525:526], scalar1=-SC0,
                                 scalar2=None, op0=ALU.mult),
                 reads=[rowsB], writes=[rowsB])
            p.op("tensor", lambda h, s:
                 h.matmul(bs.pb[4][:, 0:1], bs.cst["ones_f32"][0:1, :],
                          rows[:, 525:526], start=True, stop=True),
                 reads=[rowsB, bs.cstB], writes=[bs.pbB[4]])
            p.op("vector", lambda h, s:
                 h.tensor_copy(out=rzt[:, 512:513], in_=bs.pb[4][:, 0:1]),
                 reads=[bs.pbB[4]], writes=[rztB])

            if GLIM <= 5:
                continue
            p.fork()
            for par in (0, 1):
                if par == 1:
                    p.alt()
                for qi, pq in enumerate(OWNPOS[par]):
                    pPV, pPVB = bs.pb[6], bs.pbB[6]
                    pZ, pZB = bs.pb[5], bs.pbB[5]
                    kts = causal_ktiles(pq)
                    for step, (j, c0) in enumerate(kts):
                        psc, pscB = bs.pb[2 + step % 2], bs.pbB[2 + step % 2]
                        ncol = 512 - c0
                        klhs = kfm[:, j * 128:(j + 1) * 128]
                        krl = krope_fm[:, j * 128:(j + 1) * 128]
                        qrhs = qh[:, qi * 512 + c0: qi * 512 + 512]
                        qrr = qrh[:, TPC + qi * 512 + c0: TPC + qi * 512 + 512]
                        p.op("tensor", lambda h, s, klhs=klhs, qrhs=qrhs,
                             psc=psc, c0=c0:
                             h.matmul(psc[:, c0:512], klhs, qrhs, start=True,
                                      stop=False),
                             reads=[kfmB, qhB], writes=[pscB])
                        p.op("tensor", lambda h, s, krl=krl, qrr=qrr, psc=psc,
                             c0=c0:
                             h.matmul(psc[:, c0:512], krl, qrr, start=False,
                                      stop=True),
                             reads=[kroB, qrhB], writes=[pscB])
                        pr_sl = probs[:, (step % 2) * 512:(step % 2) * 512 + 512]
                        p.op("scalar", lambda h, s, psc=psc, pr_sl=pr_sl,
                             c0=c0:
                             h.activation(out=pr_sl[:, c0:512],
                                          in_=psc[:, c0:512], func=AF.Exp,
                                          scale=SC0, bias=rzt[:, 512:513]),
                             reads=[pscB], params=[rztB],
                             writes=[prB[step % 2]])
                        if isinstance(c0, int) and (j, c0) in \
                           [(GSLOT[pq] * 4 + r, r * 128) for r in range(4)] \
                           and GPOS[j // 4] == pq:
                            p.op("vector", lambda h, s, pr_sl=pr_sl, c0=c0:
                                 h.tensor_tensor(out=pr_sl[:, c0:c0 + 128],
                                                 in0=pr_sl[:, c0:c0 + 128],
                                                 in1=bs.cst["tri_kq"][:],
                                                 op=ALU.mult),
                                 reads=[bs.cstB], writes=[prB[step % 2]])
                        vlhs = vg[:, (hh % 4) * 2048 + j * 128:
                                  (hh % 4) * 2048 + (j + 1) * 128]
                        p.op("tensor", lambda h, s, vlhs=vlhs, pr_sl=pr_sl,
                             pPV=pPV, c0=c0, step=step, nk=len(kts):
                             h.matmul(pPV[:, c0:512], vlhs, pr_sl[:, c0:512],
                                      start=(step == 0), stop=(step == nk - 1)),
                             reads=[vgB, prB[step % 2]], writes=[pPVB])
                        p.op("tensor", lambda h, s, pr_sl=pr_sl, pZ=pZ, c0=c0,
                             step=step, nk=len(kts):
                             h.matmul(pZ[0:1, c0:512], bs.cst["ones_bf"][:, 0:1],
                                      pr_sl[:, c0:512], start=(step == 0),
                                      stop=(step == nk - 1)),
                             reads=[bs.cstB, prB[step % 2]], writes=[pZB])
                    p.op("vector", lambda h, s, pZ=pZ:
                         h.reciprocal(out=rows[0:1, 0:512],
                                      in_=pZ[0:1, :]),
                         reads=[pZB], writes=[rowsB])
                    p.op("tensor", lambda h, s:
                         h.matmul(bs.pb[4][:, 0:512],
                                  bs.cst["ones_f32"][0:1, :],
                                  rows[0:1, 0:512], start=True, stop=True),
                         reads=[rowsB, bs.cstB], writes=[bs.pbB[4]])
                    p.op("vector", lambda h, s:
                         h.tensor_copy(out=rzt[:, 0:512],
                                       in_=bs.pb[4][:, 0:512]),
                         reads=[bs.pbB[4]], writes=[rztB])
                    dst = arena[:, hh * TPC + qi * 512: hh * TPC + qi * 512 + 512]
                    p.op("vector", lambda h, s, pPV=pPV, dst=dst:
                         h.tensor_tensor(out=dst, in0=pPV[:, 0:512],
                                         in1=rzt[:, 0:512], op=ALU.mult),
                         reads=[pPVB, rztB], writes=[arenaB])
            p.join(bs.preg)
            HB = os.environ.get("L0HB", "")
            hbmap = {"kfm": kfmB, "qh": qhB, "qrh": qrhB, "sqh": sqhB,
                     "rows": rowsB, "rzt": rztB, "pr0": prB[0], "pr1": prB[1],
                     "pb2": bs.pbB[2], "pb3": bs.pbB[3], "pb4": bs.pbB[4],
                     "pb5": bs.pbB[5], "pb6": bs.pbB[6], "vg": vgB}
            if HB == "all":
                p.barrier(list(hbmap.values()))
            elif HB:
                p.barrier([hbmap[k] for k in HB.split(",")])

        emit_tap(bs, "attn", arena[:], arenaB)
        if LIM <= 4:
            return

        for m in range(16):
            a0 = (m % 2) * 2048
            wsl, wbufs = wslot(bs, a0, a0 + 2048)
            band_dma(p, "gpsimd", wsl, dram["wo"], m, wbufs)
            wB = bs.wbB[(a0, a0 + 2048)]
            for tr in range(2):
                po, poB = bs.pb[2 + tr], bs.pbB[2 + tr]
                for c in range(16):
                    rhs = arena[:, c * TPC + tr * 512: c * TPC + tr * 512 + 512]
                    p.op("tensor", lambda h, s, rhs=rhs, c=c, po=po, wsl=wsl:
                         h.matmul(po[:], wsl[:, c * 128:(c + 1) * 128], rhs,
                                  start=(c == 0), stop=(c == 15)),
                         reads=[wB, arenaB], writes=[poB])
                xsl = bs.x_fm[:, m * TPC + tr * 512: m * TPC + tr * 512 + 512]
                p.op("vector", lambda h, s, po=po, xsl=xsl:
                     h.tensor_tensor(out=xsl, in0=xsl, in1=po[:], op=ALU.add),
                     reads=[poB], writes=[bs.xB])
        p.barrier([qlrB, ckvB, kroB, kvtB[0], kvtB[1], rowsB, r2B, vgB,
                   kfmB, sqhB, qhB, qrhB, prB[0], prB[1], stgB[0],
                   rztB, gqvB, krsq2B])


EMITTERS = {"l0attn": emit_l0attn}


def core_inputs(core, blocks, x_np, w, taps=False):
    par = core % 2
    seq = core // 2
    rows = own_rows(par)
    c = host_constants()
    m = {
        "x_in": np.ascontiguousarray(x_np[seq, rows, :]),
        "ident_bf": c["ident_bf"], "ident_f32": c["ident_f32"],
        "tri_kq": c["tri_kq"], "flags": c["flags"],
        "ones_f32": c["ones_f32"],
        "parity": np.array([[par]], np.int32),
        "cos_own": np.ascontiguousarray(c["cos_all"][rows]).astype(
            ml_dtypes.bfloat16),
        "sin_own": np.ascontiguousarray(c["sin_all"][rows]).astype(
            ml_dtypes.bfloat16),
        "cos_own_fm": np.ascontiguousarray(
            np.tile(c["cos_all"][rows].T, (2, 1))).astype(ml_dtypes.bfloat16),
        "sin_own_fm": np.ascontiguousarray(
            np.tile(c["sin_all"][rows].T, (2, 1))).astype(ml_dtypes.bfloat16),
    }
    if "l0attn" in blocks:
        wq_b = np.asarray(w["L0_attn_wq_b"])
        wkv_b = np.asarray(w["L0_attn_wkv_b"])
        m["wq_a"] = host_bands(np.asarray(w["L0_attn_wq_a"]), 128)
        m["gq_a"] = np.ascontiguousarray(
            np.asarray(w["L0_attn_gq_a"]).reshape(12, 128).T)
        m["wq_b_n"] = host_bands(wq_b[:, :2048], 128)
        m["wq_b_r"] = host_bands(wq_b[:, 2048:], 64)
        m["wkv_a"] = host_bands(np.asarray(w["L0_attn_wkv_a"]), DKV)
        m["gkv_a"] = np.ascontiguousarray(
            np.asarray(w["L0_attn_gkv_a"]).reshape(4, 128).T)
        kcols = wkv_b.reshape(KVLR, H0, 2, 128)
        m["wkv_b_k"] = host_bands(
            np.ascontiguousarray(kcols[:, :, 0, :]).reshape(KVLR, H0 * 128),
            128)
        m["wkv_b_v"] = host_bands(
            np.ascontiguousarray(kcols[:, :, 1, :]).reshape(KVLR, H0 * 128),
            128)
        m["wo"] = host_bands(np.asarray(w["L0_attn_wo"]), 128)
    if "l0ffn" in blocks:
        m["gate_wT"] = host_bands(np.asarray(w["L0_ffn_gate_w"]).T, 256)
        m["up_wT"] = host_bands(np.asarray(w["L0_ffn_up_w"]).T, 256)
        m["down_wT"] = host_bands_rows(np.asarray(w["L0_ffn_down_w"]).T, 2)
        m["gate_scale"] = np.ascontiguousarray(
            np.asarray(w["L0_ffn_gate_scale"]).reshape(64, 128).T)
        m["up_scale"] = np.ascontiguousarray(
            np.asarray(w["L0_ffn_up_scale"]).reshape(64, 128).T)
        m["down_scale"] = np.ascontiguousarray(
            np.asarray(w["L0_ffn_down_scale"]).reshape(16, 128).T)
    if "l1attn" in blocks:
        w_qkv = np.asarray(w["L1_attn_w_qkv"])
        m["wqkv_q"] = host_bands(w_qkv[:, :2048], 64)
        m["wqkv_kv"] = host_bands(w_qkv[:, 2048:], 128)
        m["w_dense"] = host_bands(np.asarray(w["L1_attn_w_dense"]), 128)
    if "l1ffn" in blocks:
        m["WcT"] = host_bands(np.ascontiguousarray(
            np.asarray(w["L1_ffn_Wc"]).T), 256)
        m["genes"] = host_bands(np.asarray(w["L1_ffn_genes"]), 128)
        m["WdT"] = host_bands(np.ascontiguousarray(
            np.asarray(w["L1_ffn_Wd"]).T), 128)
        m["WuT"] = host_bands(np.ascontiguousarray(
            np.asarray(w["L1_ffn_Wu"]).T), 128)
        m["norm_w"] = np.ascontiguousarray(
            np.asarray(w["L1_ffn_norm_w"]).reshape(16, 128).T)
        m["temperature"] = np.broadcast_to(np.asarray(
            w["L1_ffn_temperature"], np.float32).reshape(1, 1),
            (128, 1)).copy()
        m["ffn_scale"] = np.broadcast_to(np.asarray(
            w["L1_ffn_scale"], np.float32).reshape(1, 1), (128, 1)).copy()
    m = {k: np.ascontiguousarray(np.asarray(v)) for k, v in m.items()}
    return m


def assemble_x(results):
    x = np.zeros((B, S, D), np.float32)
    for core in range(NCORES):
        par, seq = core % 2, core // 2
        x[seq, own_rows(par), :] = results[core]["x_out"]
    return x


_NC_CACHE = {}


def get_nc(blocks, dev_taps=()):
    key = (tuple(blocks), tuple(dev_taps))
    if key not in _NC_CACHE:
        bs = make_bs(blocks, dev_taps)
        emit_load_x(bs)
        for b in blocks:
            EMITTERS[b](bs)
        emit_store_x(bs)
        _NC_CACHE[key] = finish(bs)
    return _NC_CACHE[key]


def run_blocks(blocks, x_np, w, dev_taps=()):
    nc = get_nc(blocks, dev_taps)
    in_maps = [core_inputs(cc, blocks, x_np, w) for cc in range(NCORES)]
    res = run_bass_kernel_spmd(nc, in_maps, core_ids=list(range(NCORES)))
    return res


def kernel(**inputs):
    x = np.asarray(inputs["L0_attn_hidden_states"], np.float32)
    for blocks in (["l0attn"], ["l0ffn"], ["l1attn"], ["l1ffn"]):
        res = run_blocks(blocks, x, inputs)
        x = assemble_x(res.results)
    return x


def emit_l0ffn(bs):
    nc, p = bs.nc, bs.p
    dram = bs.dram
    emit_ln_fm(bs, bs.arena, bs.arenaB, eps=EPS_LN, kind="ln")
    arena, arenaB = bs.arena, bs.arenaB
    ISL = 1024
    NSL = I0 // ISL
    BPS = ISL // 256

    with nc.sbuf_tensor("wffn", [128, 32768], BF16) as wffn, \
         nc.sbuf_tensor("hbuf", [128, 8 * TPC], BF16) as hbuf, \
         nc.sbuf_tensor("gsil", [128, 2 * TPC], BF16) as gsil, \
         nc.sbuf_tensor("fsc", [128, 144], F32) as fsc:
        gslB = [p.buf("gsl0"), p.buf("gsl1")]
        uslB = [p.buf("usl0"), p.buf("usl1")]
        dslB = [p.buf(f"dsl{i}") for i in range(4)]
        hB = p.buf("hbuf")
        gsB = p.buf("gsil")
        fscB = p.buf("fsc")
        newb = gslB + uslB + dslB + [hB, gsB, fscB]
        p.barrier(newb)

        GS = lambda k: wffn[:, k * 4096:(k + 1) * 4096]
        US = lambda k: wffn[:, 8192 + k * 4096:8192 + (k + 1) * 4096]
        DS = lambda k: wffn[:, 16384 + k * 4096:16384 + (k + 1) * 4096]

        p.dma("sync", fsc[:, 0:64], dram["gate_scale"], writes=[fscB])
        p.dma("sync", fsc[:, 64:128], dram["up_scale"], writes=[fscB])
        p.dma("sync", fsc[:, 128:144], dram["down_scale"], writes=[fscB])

        for s in range(NSL):
            for k in range(4):
                p.op("vector", lambda h, ss, k=k: h.memset(DS(k), -128.0),
                     writes=[dslB[k]])
                accum_band_dma(p, DS(k), dram["down_wT"], s * 4 + k,
                               [dslB[k]])
            for bb in range(BPS):
                band = s * BPS + bb
                gsl, gB = GS(bb % 2), gslB[bb % 2]
                usl, uB = US(bb % 2), uslB[bb % 2]
                p.op("vector", lambda h, ss, gsl=gsl: h.memset(gsl, -128.0),
                     writes=[gB])
                accum_band_dma(p, gsl, dram["gate_wT"], band, [gB])
                p.op("vector", lambda h, ss, usl=usl: h.memset(usl, -128.0),
                     writes=[uB])
                accum_band_dma(p, usl, dram["up_wT"], band, [uB])
                for ic in range(2):
                    gidx = band * 2 + ic
                    gscale = fsc[:, gidx:gidx + 1]
                    uscale = fsc[:, 64 + gidx:65 + gidx]
                    for tr in range(2):
                        pg, pgB = bs.pb[tr], bs.pbB[tr]
                        pu, puB = bs.pb[2 + tr], bs.pbB[2 + tr]
                        rhs = None
                        for c in range(16):
                            rhs = arena[:, c * TPC + tr * 512:
                                        c * TPC + tr * 512 + 512]
                            lg = gsl[:, c * 256 + ic * 128:
                                     c * 256 + (ic + 1) * 128]
                            lu = usl[:, c * 256 + ic * 128:
                                     c * 256 + (ic + 1) * 128]
                            p.op("tensor", lambda h, ss, lg=lg, rhs=rhs, c=c,
                                 pg=pg:
                                 h.matmul(pg[:], lg, rhs, start=(c == 0),
                                          stop=(c == 15)),
                                 reads=[gB, arenaB], writes=[pgB])
                            p.op("tensor", lambda h, ss, lu=lu, rhs=rhs, c=c,
                                 pu=pu:
                                 h.matmul(pu[:], lu, rhs, start=(c == 0),
                                          stop=(c == 15)),
                                 reads=[uB, arenaB], writes=[puB])
                        gd = gsil[:, tr * TPC:tr * TPC + 512]
                        p.op("scalar", lambda h, ss, pg=pg, gd=gd,
                             gscale=gscale:
                             h.activation(out=gd, in_=pg[:], func=AF.Silu,
                                          scale=gscale),
                             reads=[pgB], params=[fscB], writes=[gsB])
                        hd = hbuf[:, (bb * 2 + ic) * TPC + tr * 512:
                                  (bb * 2 + ic) * TPC + tr * 512 + 512]
                        p.op("vector", lambda h, ss, pu=pu, hd=hd, gd=gd,
                             uscale=uscale:
                             h.scalar_tensor_tensor(out=hd, in0=pu[:],
                                                    scalar=uscale, in1=gd,
                                                    op0=ALU.mult,
                                                    op1=ALU.mult),
                             reads=[puB, gsB], params=[fscB], writes=[hB])
            for m in range(16):
                dscale = fsc[:, 128 + m:129 + m]
                for tr in range(2):
                    pd, pdB = bs.pb[4 + tr], bs.pbB[4 + tr]
                    for c in range(8):
                        lhs = DS(c // 2)[:, (c % 2) * 2048 + m * 128:
                                         (c % 2) * 2048 + (m + 1) * 128]
                        rhs = hbuf[:, c * TPC + tr * 512:
                                   c * TPC + tr * 512 + 512]
                        p.op("tensor", lambda h, ss, lhs=lhs, rhs=rhs, c=c,
                             pd=pd:
                             h.matmul(pd[:], lhs, rhs, start=(c == 0),
                                      stop=(c == 7)),
                             reads=[dslB[c // 2], hB], writes=[pdB])
                    xsl = bs.x_fm[:, m * TPC + tr * 512:
                                  m * TPC + tr * 512 + 512]
                    p.op("vector", lambda h, ss, pd=pd, xsl=xsl,
                         dscale=dscale:
                         h.scalar_tensor_tensor(out=xsl, in0=pd[:],
                                                scalar=dscale, in1=xsl,
                                                op0=ALU.mult, op1=ALU.add),
                         reads=[pdB], params=[fscB], writes=[bs.xB])
        p.barrier(newb)


EMITTERS["l0ffn"] = emit_l0ffn


def emit_l1attn(bs):
    nc, p = bs.nc, bs.p
    dram = bs.dram
    emit_ln_fm(bs, bs.arena, bs.arenaB, eps=EPS_LN, kind="ln")
    arena, arenaB = bs.arena, bs.arenaB

    with nc.sbuf_tensor("k1fm", [128, 2048], BF16) as k1fm, \
         nc.sbuf_tensor("vt1", [128, 16 * 80], BF16) as vt1, \
         nc.sbuf_tensor("qrh1", [64, 2 * TPC], BF16) as qrh1, \
         nc.sbuf_tensor("sq1", [64, 2 * TPC], BF16) as sq1, \
         nc.sbuf_tensor("kvt1", [128, 2 * 128], BF16) as kvt1, \
         nc.sbuf_tensor("stg1", [128, 128], BF16) as stg1, \
         nc.sbuf_tensor("rows1", [1, 576], F32) as rows1, \
         nc.sbuf_tensor("rzt1", [128, 513], F32) as rzt1, \
         nc.sbuf_tensor("probs1", [128, 2 * 512], BF16) as probs1, \
         nc.sbuf_tensor("qall", [128, 16 * TPC], BF16) as qall:
        k1B = p.buf("k1fm"); vt1B = p.buf("vt1"); qrB = p.buf("qrh1")
        sqB = p.buf("sq1"); kvtB = [p.buf("kvt0"), p.buf("kvt1")]
        stgB = p.buf("stg1"); rowsB = p.buf("rows1"); rztB = p.buf("rzt1")
        prB = [p.buf("p1a"), p.buf("p1b")]
        qallB = p.buf("qall")
        newb = [k1B, vt1B, qrB, sqB, kvtB[0], kvtB[1], stgB, rowsB, rztB,
                prB[0], prB[1], qallB]
        p.barrier(newb)

        kvb_sl, kvb_bufs = wslot(bs, 2048, 4096)
        kvbB = bs.wbB[(2048, 4096)]
        band_dma(p, "gpsimd", kvb_sl, dram["wqkv_kv"], 0, kvb_bufs)
        shkv = bs.shared["kv1"]
        for tt in range(8):
            pa, paB = bs.pb[tt % 2], bs.pbB[tt % 2]
            for c in range(16):
                lhs = arena[:, c * TPC + tt * 128: c * TPC + tt * 128 + 128]
                p.op("tensor", lambda h, s, lhs=lhs, c=c, pa=pa:
                     h.matmul(pa[:, 0:128], lhs,
                              kvb_sl[:, c * 128:(c + 1) * 128],
                              start=(c == 0), stop=(c == 15)),
                     reads=[kvbB, arenaB], writes=[paB])
            dst = kvt1[:, (tt % 2) * 128:(tt % 2) * 128 + 128]
            dstB = kvtB[tt % 2]
            p.op("vector", lambda h, s, pa=pa, dst=dst:
                 h.tensor_copy(out=dst[:, 64:128], in_=pa[:, 64:128]),
                 reads=[paB], writes=[dstB])
            cosl = bs.cst["cos_own"][:, tt * 32:(tt + 1) * 32]
            sinl = bs.cst["sin_own"][:, tt * 32:(tt + 1) * 32]
            krbf = probs1[:, 128:192]
            p.op("vector", lambda h, s, pa=pa, krbf=krbf:
                 h.tensor_copy(out=krbf, in_=pa[:, 0:64]),
                 reads=[paB], writes=[prB[0]])
            t4 = probs1[:, 0:128]
            for i, (tab, c0) in enumerate([(cosl, 0), (sinl, 32),
                                           (sinl, 0), (cosl, 32)]):
                p.op("vector", lambda h, s, krbf=krbf, tab=tab, c0=c0, i=i,
                     t4=t4:
                     h.tensor_tensor(out=t4[:, i * 32:(i + 1) * 32],
                                     in0=krbf[:, c0:c0 + 32],
                                     in1=tab, op=ALU.mult),
                     reads=[prB[0], bs.cstB], writes=[prB[0]])
            p.op("vector", lambda h, s, dst=dst, t4=t4:
                 h.tensor_tensor(out=dst[:, 0:32], in0=t4[:, 0:32],
                                 in1=t4[:, 32:64], op=ALU.subtract),
                 reads=[prB[0]], writes=[dstB])
            p.op("vector", lambda h, s, dst=dst, t4=t4:
                 h.tensor_tensor(out=dst[:, 32:64], in0=t4[:, 64:96],
                                 in1=t4[:, 96:128], op=ALU.add),
                 reads=[prB[0]], writes=[dstB])
            p.fork()
            for par_ in (0, 1):
                if par_ == 1:
                    p.alt()
                sdst = shkv[par_ * TPC + tt * 128:
                            par_ * TPC + tt * 128 + 128, :]
                p.dma("sync", sdst, dst, reads=[dstB])
            p.join(bs.preg)

        rtok = emit_rendezvous(bs)
        rdvB = tok_buf(rtok, "rdv1")
        for j in range(16):
            p.dma("sync", stg1[:], shkv[j * 128:(j + 1) * 128, :],
                  reads=[rdvB], writes=[stgB])
            p.op("tensor", lambda h, s, j=j:
                 h.transpose(bs.pbf[0:64, 0:128], stg1[:, 0:64],
                             bs.cst["ident_bf"][:]),
                 reads=[stgB, bs.cstB], writes=[bs.pbfB])
            p.op("vector", lambda h, s, j=j:
                 h.tensor_copy(out=k1fm[0:64, j * 128:(j + 1) * 128],
                               in_=bs.pbf[0:64, 0:128]),
                 reads=[bs.pbfB], writes=[k1B])
            p.op("tensor", lambda h, s, j=j:
                 h.transpose(bs.pbf[64:128, 128:256], stg1[:, 0:64],
                             bs.cst["ident_bf"][:]),
                 reads=[stgB, bs.cstB], writes=[bs.pbfB])
            p.op("vector", lambda h, s, j=j:
                 h.tensor_copy(out=k1fm[64:128, j * 128:(j + 1) * 128],
                               in_=bs.pbf[64:128, 128:256]),
                 reads=[bs.pbfB], writes=[k1B])
            p.op("vector", lambda h, s, j=j:
                 h.tensor_copy(out=vt1[:, j * 80:j * 80 + 64],
                               in_=stg1[:, 64:128]),
                 reads=[stgB], writes=[vt1B])
            p.op("vector", lambda h, s, j=j:
                 h.memset(vt1[:, j * 80 + 64:j * 80 + 65], 1.0),
                 writes=[vt1B])

        p.op("scalar", lambda h, s:
             h.activation(out=sq1[:, 0:2048], in_=k1fm[0:64, :],
                          func=AF.Square),
             reads=[k1B], writes=[sqB])
        for nch in range(4):
            p.op("tensor", lambda h, s, nch=nch:
                 h.matmul(bs.pb[4][0:1, :], bs.cst["ones_bf"][0:64, 0:1],
                          sq1[0:64, nch * 512:(nch + 1) * 512],
                          start=True, stop=True),
                 reads=[sqB, bs.cstB], writes=[bs.pbB[4]])
            p.op("vector", lambda h, s, nch=nch:
                 h.tensor_reduce(out=rows1[:, 512 + nch:513 + nch],
                                 in_=bs.pb[4][0:1, :], axis=AXX, op=ALU.max),
                 reads=[bs.pbB[4]], writes=[rowsB])
        p.op("vector", lambda h, s:
             h.tensor_reduce(out=rows1[:, 516:517], in_=rows1[:, 512:516],
                             axis=AXX, op=ALU.max),
             reads=[rowsB], writes=[rowsB])
        p.op("scalar", lambda h, s:
             h.activation(out=rows1[:, 517:518], in_=rows1[:, 516:517],
                          func=AF.Ln),
             reads=[rowsB], writes=[rowsB])

        for hh in range(H1):
            a1 = (hh % 2) * 1024
            qb, qb_bufs = wslot(bs, a1, a1 + 1024)
            qbB = bs.wbB[(a1, a1 + 1024)]
            band_dma(p, "gpsimd", qb, dram["wqkv_q"], hh, qb_bufs)
            for tr in range(2):
                pn, pnB = bs.pb[2 + tr], bs.pbB[2 + tr]
                for c in range(16):
                    rhs = arena[:, c * TPC + tr * 512: c * TPC + tr * 512 + 512]
                    p.op("tensor", lambda h, s, rhs=rhs, c=c, pn=pn, qb=qb:
                         h.matmul(pn[0:64, :], qb[:, c * 64:(c + 1) * 64],
                                  rhs, start=(c == 0), stop=(c == 15)),
                         reads=[qbB, arenaB], writes=[pnB])
                p.op("vector", lambda h, s, pn=pn, tr=tr:
                     h.tensor_copy(out=qrh1[:, tr * 512:(tr + 1) * 512],
                                   in_=pn[0:64, :]),
                     reads=[pnB], writes=[qrB])
                p.op("scalar", lambda h, s, pn=pn, tr=tr:
                     h.activation(out=sq1[:, tr * 512:tr * 512 + 512],
                                  in_=pn[0:64, :], func=AF.Square),
                     reads=[pnB], writes=[sqB])
                p.op("tensor", lambda h, s, tr=tr:
                     h.matmul(bs.pb[4][0:1, :], bs.cst["ones_bf"][0:64, 0:1],
                              sq1[0:64, tr * 512:tr * 512 + 512],
                              start=True, stop=True),
                     reads=[sqB, bs.cstB], writes=[bs.pbB[4]])
                p.op("vector", lambda h, s, tr=tr:
                     h.tensor_reduce(out=rows1[:, 520 + tr:521 + tr],
                                     in_=bs.pb[4][0:1, :], axis=AXX,
                                     op=ALU.max),
                     reads=[bs.pbB[4]], writes=[rowsB])
            qdst = qall[(hh % 2) * 64:(hh % 2) * 64 + 64,
                        (hh // 2) * TPC:(hh // 2) * TPC + TPC]
            for tr in range(2):
                cosl = bs.cst["cos_own_fm"][:, tr * 512:(tr + 1) * 512]
                sinl = bs.cst["sin_own_fm"][:, tr * 512:(tr + 1) * 512]
                raw = qrh1[0:64, tr * 512:tr * 512 + 512]
                o1 = qdst[0:32, tr * 512:tr * 512 + 512]
                o2 = qdst[32:64, tr * 512:tr * 512 + 512]
                ta = sq1[0:64, 0:512]
                tbs = sq1[0:64, 512:1024]
                p.op("vector", lambda h, s, raw=raw, ta=ta, cosl=cosl:
                     h.tensor_tensor(out=ta, in0=raw, in1=cosl, op=ALU.mult),
                     reads=[qrB, bs.cstB], writes=[sqB])
                p.op("vector", lambda h, s, raw=raw, tbs=tbs, sinl=sinl:
                     h.tensor_tensor(out=tbs[0:32, :], in0=raw[32:64, :],
                                     in1=sinl[32:64, :], op=ALU.mult),
                     reads=[qrB, bs.cstB], writes=[sqB])
                p.op("vector", lambda h, s, raw=raw, tbs=tbs, sinl=sinl:
                     h.tensor_tensor(out=tbs[32:64, :], in0=raw[0:32, :],
                                     in1=sinl[0:32, :], op=ALU.mult),
                     reads=[qrB, bs.cstB], writes=[sqB])
                p.op("vector", lambda h, s, o1=o1, ta=ta, tbs=tbs:
                     h.tensor_tensor(out=o1, in0=ta[0:32, :],
                                     in1=tbs[0:32, :], op=ALU.subtract),
                     reads=[sqB], writes=[qallB])
                p.op("vector", lambda h, s, o2=o2, ta=ta, tbs=tbs:
                     h.tensor_tensor(out=o2, in0=ta[32:64, :],
                                     in1=tbs[32:64, :], op=ALU.add),
                     reads=[sqB], writes=[qallB])
            p.op("vector", lambda h, s, hh=hh:
                 h.tensor_reduce(out=rows1[:, 528 + hh:529 + hh],
                                 in_=rows1[:, 520:522], axis=AXX, op=ALU.max),
                 reads=[rowsB], writes=[rowsB])

        for hh in range(H1):
            p.op("vector", lambda h, s, hh=hh:
                 h.tensor_copy(out=rows1[:, 522:523],
                               in_=rows1[:, 528 + hh:529 + hh]),
                 reads=[rowsB], writes=[rowsB])
            p.op("scalar", lambda h, s:
                 h.activation(out=rows1[:, 523:524], in_=rows1[:, 522:523],
                              func=AF.Ln),
                 reads=[rowsB], writes=[rowsB])
            p.op("vector", lambda h, s:
                 h.tensor_tensor(out=rows1[:, 524:525],
                                 in0=rows1[:, 523:524],
                                 in1=rows1[:, 517:518], op=ALU.add),
                 reads=[rowsB], writes=[rowsB])
            p.op("scalar", lambda h, s:
                 h.activation(out=rows1[:, 524:525], in_=rows1[:, 524:525],
                              func=AF.Exp, scale=0.5),
                 reads=[rowsB], writes=[rowsB])
            p.op("vector", lambda h, s:
                 h.tensor_scalar(out=rows1[:, 524:525],
                                 in0=rows1[:, 524:525], scalar1=-SC1,
                                 scalar2=None, op0=ALU.mult),
                 reads=[rowsB], writes=[rowsB])
            p.op("tensor", lambda h, s:
                 h.matmul(bs.pb[4][:, 0:1], bs.cst["ones_f32"][0:1, :],
                          rows1[:, 524:525], start=True, stop=True),
                 reads=[rowsB, bs.cstB], writes=[bs.pbB[4]])
            p.op("vector", lambda h, s:
                 h.tensor_copy(out=rzt1[:, 512:513], in_=bs.pb[4][:, 0:1]),
                 reads=[bs.pbB[4]], writes=[rztB])

            p.fork()
            for par in (0, 1):
                if par == 1:
                    p.alt()
                for qi, pq in enumerate(OWNPOS[par]):
                    pPV, pPVB = bs.pb[6], bs.pbB[6]
                    kts = causal_ktiles(pq)
                    for step, (j, c0) in enumerate(kts):
                        psc, pscB = bs.pb[2 + step % 2], bs.pbB[2 + step % 2]
                        klhs = k1fm[(hh % 2) * 64:(hh % 2) * 64 + 64,
                                    j * 128:(j + 1) * 128]
                        qrr = qall[(hh % 2) * 64:(hh % 2) * 64 + 64,
                                   (hh // 2) * TPC + qi * 512 + c0:
                                   (hh // 2) * TPC + qi * 512 + 512]
                        p.op("tensor", lambda h, s, klhs=klhs, qrr=qrr,
                             psc=psc, c0=c0:
                             h.matmul(psc[:, c0:512], klhs, qrr, start=True,
                                      stop=True),
                             reads=[k1B, qallB], writes=[pscB])
                        pr_sl = probs1[:, (step % 2) * 512:
                                       (step % 2) * 512 + 512]
                        p.op("scalar", lambda h, s, psc=psc, pr_sl=pr_sl,
                             c0=c0:
                             h.activation(out=pr_sl[:, c0:512],
                                          in_=psc[:, c0:512], func=AF.Exp,
                                          scale=SC1, bias=rzt1[:, 512:513]),
                             reads=[pscB], params=[rztB],
                             writes=[prB[step % 2]])
                        if GPOS[j // 4] == pq:
                            p.op("vector", lambda h, s, pr_sl=pr_sl, c0=c0:
                                 h.tensor_tensor(out=pr_sl[:, c0:c0 + 128],
                                                 in0=pr_sl[:, c0:c0 + 128],
                                                 in1=bs.cst["tri_kq"][:],
                                                 op=ALU.mult),
                                 reads=[bs.cstB], writes=[prB[step % 2]])
                        vlhs = vt1[:, j * 80:j * 80 + 65]
                        p.op("tensor", lambda h, s, vlhs=vlhs, pr_sl=pr_sl,
                             pPV=pPV, c0=c0, step=step, nk=len(kts):
                             h.matmul(pPV[0:65, c0:512], vlhs,
                                      pr_sl[:, c0:512],
                                      start=(step == 0), stop=(step == nk - 1)),
                             reads=[vt1B, prB[step % 2]], writes=[pPVB])
                    p.op("vector", lambda h, s, pPV=pPV:
                         h.reciprocal(out=rows1[0:1, 0:512],
                                      in_=pPV[64:65, 0:512]),
                         reads=[pPVB], writes=[rowsB])
                    p.op("tensor", lambda h, s:
                         h.matmul(bs.pb[4][0:64, 0:512],
                                  bs.cst["ones_f32"][0:1, 0:64],
                                  rows1[0:1, 0:512], start=True, stop=True),
                         reads=[rowsB, bs.cstB], writes=[bs.pbB[4]])
                    p.op("vector", lambda h, s:
                         h.tensor_copy(out=rzt1[0:64, 0:512],
                                       in_=bs.pb[4][0:64, 0:512]),
                         reads=[bs.pbB[4]], writes=[rztB])
                    dst = arena[(hh % 2) * 64:(hh % 2) * 64 + 64,
                                (hh // 2) * TPC + qi * 512:
                                (hh // 2) * TPC + qi * 512 + 512]
                    p.op("vector", lambda h, s, pPV=pPV, dst=dst:
                         h.tensor_tensor(out=dst, in0=pPV[0:64, 0:512],
                                         in1=rzt1[0:64, 0:512], op=ALU.mult),
                         reads=[pPVB, rztB], writes=[arenaB])
            p.join(bs.preg)

        for m in range(16):
            a0 = 4096 + (m % 2) * 2048
            wsl, wbufs = wslot(bs, a0, a0 + 2048)
            wB = bs.wbB[(a0, a0 + 2048)]
            band_dma(p, "gpsimd", wsl, dram["w_dense"], m, wbufs)
            for tr in range(2):
                po, poB = bs.pb[4 + tr], bs.pbB[4 + tr]
                for c in range(16):
                    rhs = arena[:, c * TPC + tr * 512: c * TPC + tr * 512 + 512]
                    p.op("tensor", lambda h, s, rhs=rhs, c=c, po=po, wsl=wsl:
                         h.matmul(po[:], wsl[:, c * 128:(c + 1) * 128], rhs,
                                  start=(c == 0), stop=(c == 15)),
                         reads=[wB, arenaB], writes=[poB])
                xsl = bs.x_fm[:, m * TPC + tr * 512: m * TPC + tr * 512 + 512]
                p.op("vector", lambda h, s, po=po, xsl=xsl:
                     h.tensor_tensor(out=xsl, in0=xsl, in1=po[:], op=ALU.add),
                     reads=[poB], writes=[bs.xB])
        p.barrier(newb)


EMITTERS["l1attn"] = emit_l1attn


def emit_l1ffn(bs):
    nc, p = bs.nc, bs.p
    dram = bs.dram

    with nc.sbuf_tensor("nwg", [128, 16], F32) as nwg:
        nwB = p.buf("nwg")
        p.barrier([nwB])
        p.dma("sync", nwg[:], dram["norm_w"], writes=[nwB])
        emit_ln_fm(bs, bs.arena, bs.arenaB, eps=EPS_RMS1, kind="rms",
                   gamma=nwg, gammaB=nwB)
    arena, arenaB = bs.arena, bs.arenaB

    with nc.sbuf_tensor("wcb", [128, 4096], BF16) as wcb, \
         nc.sbuf_tensor("ltile", [128, 4096], F32) as ltile, \
         nc.sbuf_tensor("lwork", [128, 4096], BF16) as lwork, \
         nc.sbuf_tensor("genesb", [128, 8192], BF16) as genesb, \
         nc.sbuf_tensor("wdb", [128, 4096], BF16) as wdb, \
         nc.sbuf_tensor("wub", [128, 4096], BF16) as wub, \
         nc.sbuf_tensor("probsT", [128, 4096], BF16) as probsT, \
         nc.sbuf_tensor("exprs", [128, 256], BF16) as exprs, \
         nc.sbuf_tensor("dnf", [128, 2048], BF16) as dnf, \
         nc.sbuf_tensor("upf", [128, 2048], BF16) as upf, \
         nc.sbuf_tensor("m8", [128, 32], F32) as m8, \
         nc.sbuf_tensor("frow", [128, 8], F32) as frow, \
         nc.sbuf_tensor("srow", [128, 8], F32) as srow:
        wcB = p.buf("wcb"); ltB = p.buf("ltile"); lwB = p.buf("lwork")
        geB = p.buf("genesb"); wdB = p.buf("wdb"); wuB = p.buf("wub")
        pTB = p.buf("probsT"); exB = p.buf("exprs")
        dnB = p.buf("dnf"); upB = p.buf("upf")
        m8B = p.buf("m8"); frB = p.buf("frow"); srB = p.buf("srow")
        newb = [wcB, ltB, lwB, geB, wdB, wuB, pTB, exB, dnB, upB, m8B, frB,
                srB]
        p.barrier(newb)

        p.op("vector", lambda h, s: h.memset(frow[:], 0.0), writes=[frB])
        p.op("vector", lambda h, s: h.memset(m8[:], 0.0), writes=[m8B])
        p.dma("sync", srow[:, 0:1], dram["temperature"], writes=[srB])
        p.dma("sync", srow[:, 1:2], dram["ffn_scale"], writes=[srB])
        p.op("vector", lambda h, s:
             h.tensor_scalar(out=srow[:, 3:4], in0=srow[:, 0:1],
                             scalar1=-0.1, scalar2=None, op0=ALU.add),
             reads=[srB], writes=[srB])
        p.op("vector", lambda h, s:
             h.scalar_tensor_tensor(out=srow[:, 2:3], in0=srow[:, 0:1],
                                    scalar=0.1, in1=srow[:, 3:4],
                                    op0=ALU.is_ge, op1=ALU.mult),
             params=[srB], writes=[srB])
        p.op("vector", lambda h, s:
             h.tensor_scalar(out=srow[:, 2:3], in0=srow[:, 2:3],
                             scalar1=0.1, scalar2=None, op0=ALU.add),
             params=[srB], writes=[srB])
        p.op("vector", lambda h, s:
             h.reciprocal(out=frow[:, 0:1], in_=srow[:, 2:3]),
             params=[srB], writes=[frB])
        p.op("vector", lambda h, s:
             h.tensor_copy(out=frow[:, 1:2], in_=srow[:, 1:2]),
             params=[srB], writes=[frB])
        invtb = frow[:, 0:1]
        scaleb = frow[:, 1:2]

        for b in range(2):
            band_dma(p, "gpsimd", genesb[:, b * 4096:(b + 1) * 4096],
                     dram["genes"], b, [geB])
        for b in range(16):
            band_dma(p, "gpsimd", wdb[:, b * 256:(b + 1) * 256],
                     dram["WdT"], b, [wdB])
            band_dma(p, "gpsimd", wub[:, b * 256:(b + 1) * 256],
                     dram["WuT"], b, [wuB])

        for tt in range(8):
            for gs in range(16):
                wslt = wcb[:, 0:4096]
                band_dma(p, "gpsimd", wslt, dram["WcT"], gs, [wcB])
                pl, plB = bs.pb[2 + gs % 2], bs.pbB[2 + gs % 2]
                for c in range(16):
                    lhs = arena[:, c * TPC + tt * 128: c * TPC + tt * 128 + 128]
                    p.op("tensor", lambda h, s, lhs=lhs, c=c, pl=pl,
                         wslt=wslt:
                         h.matmul(pl[:, 0:256], lhs,
                                  wslt[:, c * 256:c * 256 + 256],
                                  start=(c == 0), stop=(c == 15)),
                         reads=[wcB, arenaB], writes=[plB])
                p.op("vector", lambda h, s, pl=pl, gs=gs:
                     h.tensor_copy(out=ltile[:, gs * 256:(gs + 1) * 256],
                                   in_=pl[:, 0:256]),
                     reads=[plB], writes=[ltB])
            TAPT = int(os.environ.get("L1FTAP", "-1"))
            if tt == TAPT:
                p.dma("sync", bs.taps["lt_raw"], ltile[:], reads=[ltB])
            p.op("vector", lambda h, s:
                 h.max(out=m8[:, 0:8], in_=ltile[:]),
                 reads=[ltB], writes=[m8B])
            p.op("vector", lambda h, s:
                 h.tensor_scalar(out=frow[:, 2:3], in0=m8[:, 0:1],
                                 scalar1=invtb, scalar2=-1.0, op0=ALU.mult,
                                 op1=ALU.mult),
                 params=[m8B, frB], writes=[frB])
            p.op("scalar", lambda h, s:
                 h.activation(out=lwork[:], in_=ltile[:], func=AF.Exp,
                              scale=invtb, bias=frow[:, 2:3]),
                 reads=[ltB], params=[frB], writes=[lwB])
            for r in range(4):
                if r > 0:
                    p.op("vector", lambda h, s, r=r:
                         h.max(out=m8[:, r * 8:(r + 1) * 8], in_=ltile[:]),
                         reads=[ltB], writes=[m8B])
                p.op("vector", lambda h, s, r=r:
                     h.match_replace(out=ltile[:],
                                     in_to_replace=m8[:, r * 8:(r + 1) * 8],
                                     in_values=ltile[:], imm_value=-1e30),
                     params=[m8B], writes=[ltB])
            p.op("vector", lambda h, s:
                 h.scalar_tensor_tensor(out=lwork[:], in0=ltile[:],
                                        scalar=-1e29, in1=lwork[:],
                                        op0=ALU.is_lt, op1=ALU.mult,
                                        accum_out=frow[:, 3:4]),
                 reads=[ltB], writes=[lwB, frB])
            p.op("vector", lambda h, s:
                 h.reciprocal(out=frow[:, 4:5], in_=frow[:, 3:4]),
                 params=[frB], writes=[frB])
            p.op("vector", lambda h, s:
                 h.tensor_scalar(out=lwork[:], in0=lwork[:],
                                 scalar1=frow[:, 4:5], scalar2=None,
                                 op0=ALU.mult),
                 params=[frB], writes=[lwB])
            if tt == TAPT:
                p.dma("sync", bs.taps["lt_zap"], ltile[:], reads=[ltB])
                p.dma("sync", bs.taps["lwork_n"], lwork[:], reads=[lwB])
                p.dma("sync", bs.taps["m8v"], m8[:], reads=[m8B])
                p.dma("sync", bs.taps["frowv"], frow[:], reads=[frB])
                p.dma("sync", bs.taps["srowv"], srow[:], reads=[srB])
            for gc in range(32):
                p.op("tensor", lambda h, s, gc=gc:
                     h.transpose(bs.pbf[:, 0:128],
                                 lwork[:, gc * 128:(gc + 1) * 128],
                                 bs.cst["ident_bf"][:]),
                     reads=[lwB, bs.cstB], writes=[bs.pbfB])
                p.op("vector", lambda h, s, gc=gc:
                     h.tensor_copy(out=probsT[:, gc * 128:(gc + 1) * 128],
                                   in_=bs.pbf[:, 0:128]),
                     reads=[bs.pbfB], writes=[pTB])
            for mc in range(2):
                pe_, peB = bs.pb[4 + mc], bs.pbB[4 + mc]
                for gc in range(32):
                    lhs = genesb[:, mc * 4096 + gc * 128:
                                 mc * 4096 + (gc + 1) * 128]
                    p.op("tensor", lambda h, s, lhs=lhs, gc=gc, pe_=pe_:
                         h.matmul(pe_[:, 0:128], lhs,
                                  probsT[:, gc * 128:(gc + 1) * 128],
                                  start=(gc == 0), stop=(gc == 31)),
                         reads=[geB, pTB], writes=[peB])
                p.op("vector", lambda h, s, pe_=pe_, mc=mc:
                     h.tensor_copy(out=exprs[:, mc * 128:(mc + 1) * 128],
                                   in_=pe_[:, 0:128]),
                     reads=[peB], writes=[exB])
            for m in range(16):
                pd, pdB = bs.pb[4], bs.pbB[4]
                pu, puB = bs.pb[5], bs.pbB[5]
                for c in range(2):
                    el = exprs[:, c * 128:(c + 1) * 128]
                    p.op("tensor", lambda h, s, el=el, c=c, m=m, pd=pd:
                         h.matmul(pd[:, 0:128], wdb[:, m * 256 + c * 128:
                                                     m * 256 + (c + 1) * 128],
                                  el, start=(c == 0), stop=(c == 1)),
                         reads=[wdB, exB], writes=[pdB])
                    p.op("tensor", lambda h, s, el=el, c=c, m=m, pu=pu:
                         h.matmul(pu[:, 0:128], wub[:, m * 256 + c * 128:
                                                     m * 256 + (c + 1) * 128],
                                  el, start=(c == 0), stop=(c == 1)),
                         reads=[wuB, exB], writes=[puB])
                p.op("scalar", lambda h, s, pd=pd, m=m:
                     h.activation(out=dnf[:, m * 128:(m + 1) * 128],
                                  in_=pd[:, 0:128], func=AF.Tanh),
                     reads=[pdB], writes=[dnB])
                p.op("vector", lambda h, s, pu=pu, m=m:
                     h.tensor_copy(out=upf[:, m * 128:(m + 1) * 128],
                                   in_=pu[:, 0:128]),
                     reads=[puB], writes=[upB])
            for m in range(16):
                dn = dnf[:, m * 128:(m + 1) * 128]
                up = upf[:, m * 128:(m + 1) * 128]
                xnsl = arena[:, m * TPC + tt * 128: m * TPC + tt * 128 + 128]
                xsl = bs.x_fm[:, m * TPC + tt * 128: m * TPC + tt * 128 + 128]
                p.op("vector", lambda h, s, dn=dn, xnsl=xnsl:
                     h.tensor_tensor(out=dn, in0=dn, in1=xnsl, op=ALU.mult),
                     reads=[arenaB], writes=[dnB])
                p.op("vector", lambda h, s, dn=dn, up=up:
                     h.tensor_tensor(out=dn, in0=dn, in1=up, op=ALU.mult),
                     reads=[upB], writes=[dnB])
                p.op("vector", lambda h, s, dn=dn, xsl=xsl:
                     h.scalar_tensor_tensor(out=xsl, in0=dn, scalar=scaleb,
                                            in1=xsl, op0=ALU.mult,
                                            op1=ALU.add),
                     reads=[dnB], params=[frB], writes=[bs.xB])
        p.barrier(newb)


EMITTERS["l1ffn"] = emit_l1ffn
